# version 05
# baseline (speedup 1.0000x reference)
import contextlib
import numpy as np
import concourse.bass as bass
import concourse.mybir as mybir
from concourse.bass_utils import run_bass_kernel_spmd

F32 = mybir.dt.float32
BF16 = mybir.dt.bfloat16
I32 = mybir.dt.int32
Alu = mybir.AluOpType
Act = mybir.ActivationFunctionType
AX = mybir.AxisListType

S = 2048
H = 2048
NH = 16
FI = 8192
NTT = S // 128
QL = 512

_D = np.sqrt(np.float32(128.0)).astype(np.float32)
_INVD = 1.0 / np.float64(_D)
CW_HI = float(np.float32(_INVD))
CW_LO = float(np.float32(np.float64(_INVD) - np.float64(np.float32(_INVD))))
EPS_LN = float(np.float32(1e-5))
INV_H = 1.0 / 2048.0

C_448, C_NEG8 = 0, 1
CST_VALS = [448.0, -8.0]


class Sem:
    def __init__(self, name):
        self.name = name
        self.h = None
        self.n = 0

    def inc(self, k):
        self.n += k
        return self.n


class Prog:
    ENGINES = ("sync", "vector", "scalar", "tensor", "gpsimd")

    def __init__(self):
        self.q = {e: [] for e in self.ENGINES}

    def on(self, eng, fn, waits=(), inc=None, inc_n=1):
        w = [(s, v) for (s, v) in waits if s is not None and v > 0]
        self.q[eng].append((fn, w, inc, inc_n))
        return inc.inc(inc_n) if inc is not None else None


DEBUG_SPILLS = ()
SAFE_V = True
SAFE_S = False
SAFE_DMA = False


def build_program():
    nc = bass.Bass("TRN2", target_bir_lowering=False, debug=False,
                   detect_race_conditions=False)
    P = Prog()

    def _internal(name, shape, dtype):
        kind = "ExternalOutput" if name in DEBUG_SPILLS else "Internal"
        return nc.dram_tensor(name, shape, dtype, kind=kind).ap()

    d_x = nc.dram_tensor("x", [S, H], F32, kind="ExternalInput").ap()
    d_wqkv = nc.dram_tensor("wqkv", [H, 3 * H], F32, kind="ExternalInput").ap()
    d_wo = nc.dram_tensor("wo", [H, H], F32, kind="ExternalInput").ap()
    d_gw = nc.dram_tensor("gw", [FI, H], F32, kind="ExternalInput").ap()
    d_uw = nc.dram_tensor("uw", [FI, H], F32, kind="ExternalInput").ap()
    d_dw = nc.dram_tensor("dw", [H, FI], F32, kind="ExternalInput").ap()
    d_cst = nc.dram_tensor("cst", [128, len(CST_VALS)], F32, kind="ExternalInput").ap()
    d_ones = nc.dram_tensor("ones1", [1, 128], F32, kind="ExternalInput").ap()
    d_idf = nc.dram_tensor("identf", [128, 128], F32, kind="ExternalInput").ap()
    d_out = nc.dram_tensor("out", [S, H], F32, kind="ExternalOutput").ap()

    d_wqkvB = _internal("wqkvB", [48, 16, 128, 128], F32)
    d_swo_hi = _internal("swo_hi", [16, 4, 128, 512], BF16)
    d_swo_lo = _internal("swo_lo", [16, 4, 128, 512], BF16)
    d_sgw_hi = _internal("sgw_hi", [64, 4, 128, 512], BF16)
    d_sgw_lo = _internal("sgw_lo", [64, 4, 128, 512], BF16)
    d_suw_hi = _internal("suw_hi", [64, 4, 128, 512], BF16)
    d_suw_lo = _internal("suw_lo", [64, 4, 128, 512], BF16)
    d_sdw_hi = _internal("sdw_hi", [16, 16, 128, 512], BF16)
    d_sdw_lo = _internal("sdw_lo", [16, 16, 128, 512], BF16)
    d_gwT_hi = _internal("gwT_hi", [64, 128, 2048], BF16)
    d_gwT_lo = _internal("gwT_lo", [64, 128, 2048], BF16)
    d_uwT_hi = _internal("uwT_hi", [64, 128, 2048], BF16)
    d_uwT_lo = _internal("uwT_lo", [64, 128, 2048], BF16)
    d_dwT_hi = _internal("dwT_hi", [64, 16, 128, 128], BF16)
    d_dwT_lo = _internal("dwT_lo", [64, 16, 128, 128], BF16)
    d_xqT_hi = _internal("xqT_hi", [NTT, 4, 128, 512], BF16)
    d_xqT_lo = _internal("xqT_lo", [NTT, 4, 128, 512], BF16)
    d_hqT_hi = _internal("hqT_hi", [NTT, 16, 128, 512], BF16)
    d_hqT_lo = _internal("hqT_lo", [NTT, 16, 128, 512], BF16)
    d_sval_hi = _internal("sval_hi", [NTT, NH, 128, 128], BF16)
    d_sval_lo = _internal("sval_lo", [NTT, NH, 128, 128], BF16)
    d_sattnB = _internal("sattnB", [NTT, 4, 128, 512], F32)

    sT, sV, sS, ph = (Sem(n) for n in ("sT", "sV", "sS", "ph"))
    all_sems = [sT, sV, sS, ph]
    _streams = {}

    def stream(name):
        if name not in _streams:
            s_ = Sem(name)
            _streams[name] = s_
            all_sems.append(s_)
        return _streams[name]

    def dma_in(sem, out_ap, in_ap, waits=(), q="sync"):
        return P.on(q, lambda e, o=out_ap, i=in_ap: e.dma_start(out=o, in_=i),
                    waits, sem, 16)

    def dma_wr(sem, out_ap, in_ap, waits=()):
        return P.on("gpsimd", lambda e, o=out_ap, i=in_ap:
                    e.dma_start(out=o, in_=i), waits, sem, 16)

    def onV(fn, waits=()):
        w = list(waits)
        if SAFE_V:
            w.append((sV, sV.n))
        return P.on("vector", fn, w, sV, 1)

    def onS(fn, waits=()):
        w = list(waits)
        if SAFE_S:
            w.append((sS, sS.n))
        return P.on("scalar", fn, w, sS, 1)

    def onT(fn, waits=()):
        return P.on("tensor", fn, waits, sT, 1)

    def barrier():
        dw_ = [(s_, s_.n) for s_ in _streams.values()]
        for e in Prog.ENGINES:
            P.on(e, lambda en: en.nop(), dw_, ph, 1)
        tgt = ph.n
        for e in Prog.ENGINES:
            P.on(e, None, [(ph, tgt)])

    outer = contextlib.ExitStack()
    with outer:
        def sb(name, shape, dtype=F32, st=None):
            return (st or outer).enter_context(nc.sbuf_tensor(name, shape, dtype))

        cst_t = sb("cst_t", [128, len(CST_VALS)])
        r448 = sb("r448", [128, 1])
        ones_t = sb("ones_t", [1, 128])
        identF = sb("identF", [128, 128], F32)
        ident = sb("ident", [128, 128], BF16)
        q_sp = sb("q_sp", [128, 1])
        q_ya = sb("q_ya", [128, QL])
        q_v = sb("q_v", [128, QL])
        q_eb = sb("q_eb", [128, QL], I32)
        q_ib = sb("q_ib", [128, QL], I32)
        q_qb = sb("q_qb", [128, QL], I32)
        q_hi = sb("q_hi", [128, QL], BF16)
        q_lo = sb("q_lo", [128, QL], BF16)
        q_m = sb("q_m", [128, QL // 32])
        q_s = sb("q_s", [128, QL // 32])
        q_rs = sb("q_rs", [128, QL // 32])

        rCST = stream("rCST")
        dma_in(rCST, cst_t[:], d_cst)
        dma_in(rCST, ones_t[:], d_ones)
        id_rdy = dma_in(rCST, identF[:], d_idf)
        sI = onS(lambda e: e.copy(out=ident[:], in_=identF[:]), [(rCST, id_rdy)])
        onV(lambda e: e.tensor_scalar(
            out=q_sp[:], in0=cst_t[:, C_448:C_448 + 1], scalar1=1.0, scalar2=None,
            op0=Alu.mult), [(rCST, id_rdy)])
        r448_rdy = onV(lambda e: e.reciprocal(out=r448[:],
                                              in_=cst_t[:, C_448:C_448 + 1]))

        P_A = outer.enter_context(nc.psum_tensor("P_A", [128, 2048], F32))
        P_B = outer.enter_context(nc.psum_tensor("P_B", [128, 2048], F32))

        rPREP = stream("rPREP")
        wQ = stream("wQ")
        qws = {"w": []}

        def emit_quant(src_ap, ws, src_waits=()):
            L = QL
            nb = L // 32
            w_src = list(src_waits)
            qt_, qu_, qf_ = ws["t"], ws["u"], ws["f"]
            sya = onS(lambda e, a=src_ap: e.activation(
                out=q_ya[:], in_=a, func=Act.Abs), w_src + [(sV, sV.n)])
            onV(lambda e, a=src_ap: e.tensor_reduce(
                out=q_m[:], in_=a.rearrange("p (b k) -> p b k", k=32),
                axis=AX.X, op=Alu.max, apply_absolute_value=True), w_src)
            onV(lambda e: e.tensor_scalar(
                out=q_s[:], in0=q_m[:], scalar1=r448[:], scalar2=1e-30,
                op0=Alu.mult, op1=Alu.max), [(sV, r448_rdy)])
            onV(lambda e: e.tensor_scalar(out=q_sp[:], in0=q_s[:, 0:1],
                                          scalar1=1.0, scalar2=None,
                                          op0=Alu.mult))
            vrs = onV(lambda e: e.reciprocal(out=q_rs[:], in_=q_s[:]))
            vt = onV(lambda e, qt_=qt_: e.tensor_tensor(
                out=qt_.rearrange("p (b k) -> p b k", k=32),
                in0=q_ya[:].rearrange("p (b k) -> p b k", k=32),
                in1=q_rs[:].unsqueeze(2).broadcast_to([128, nb, 32]),
                op=Alu.mult), [(sS, sya), (sV, vrs)])
            onV(lambda e, qt_=qt_: e.tensor_scalar(
                out=q_eb[:], in0=qt_.bitcast(I32), scalar1=23, scalar2=None,
                op0=Alu.logical_shift_right))
            onV(lambda e: e.tensor_scalar(
                out=q_eb[:], in0=q_eb[:], scalar1=121.0, scalar2=None,
                op0=Alu.max))
            onV(lambda e: e.tensor_scalar(
                out=q_ib[:], in0=q_eb[:], scalar1=-1.0, scalar2=257.0,
                op0=Alu.mult, op1=Alu.add))
            onV(lambda e: e.tensor_scalar(
                out=q_ib[:], in0=q_ib[:], scalar1=23, scalar2=None,
                op0=Alu.logical_shift_left))
            onV(lambda e: e.tensor_scalar(
                out=q_qb[:], in0=q_eb[:], scalar1=3.0, scalar2=None,
                op0=Alu.subtract))
            onV(lambda e: e.tensor_scalar(
                out=q_qb[:], in0=q_qb[:], scalar1=23, scalar2=None,
                op0=Alu.logical_shift_left))
            onV(lambda e, qt_=qt_, qu_=qu_: e.tensor_tensor(
                out=qu_, in0=qt_, in1=q_ib[:].bitcast(F32), op=Alu.mult))
            onV(lambda e, qu_=qu_: e.tensor_copy(out=q_eb[:], in_=qu_))
            onV(lambda e, qu_=qu_, qf_=qf_: e.tensor_tensor(
                out=qf_, in0=qu_, in1=q_eb[:], op=Alu.subtract))
            onV(lambda e, qf_=qf_: e.scalar_tensor_tensor(
                out=q_v[:], in0=qf_, scalar=0.5, in1=q_eb[:],
                op0=Alu.is_ge, op1=Alu.add))
            onV(lambda e, qu_=qu_: e.tensor_tensor(
                out=qu_, in0=q_v[:], in1=q_qb[:].bitcast(F32), op=Alu.mult))
            onV(lambda e, qt_=qt_, qu_=qu_: e.tensor_tensor(
                out=qt_.rearrange("p (b k) -> p b k", k=32),
                in0=qu_.rearrange("p (b k) -> p b k", k=32),
                in1=q_s[:].unsqueeze(2).broadcast_to([128, nb, 32]),
                op=Alu.mult))
            ssg = onS(lambda e, a=src_ap: e.activation(
                out=q_ya[:], in_=a, func=Act.Sign), [(sV, vt)])
            vdq = onV(lambda e, qt_=qt_, qf_=qf_: e.tensor_tensor(
                out=qf_, in0=qt_, in1=q_ya[:], op=Alu.mult), [(sS, ssg)])
            shi = onS(lambda e, qf_=qf_: e.copy(out=q_hi[:], in_=qf_),
                      [(sV, vdq)] + list(qws["w"]))
            vlo = onV(lambda e, qf_=qf_: e.tensor_tensor(
                out=q_lo[:], in0=qf_, in1=q_hi[:], op=Alu.subtract),
                [(sS, shi)])
            return shi, vlo

        prep_jobs = ([("wo", i) for i in range(16)]
                     + [("gw", i) for i in range(64)]
                     + [("uw", i) for i in range(64)]
                     + [("dw", i) for i in range(64)])
        prep_state = {"i": 0, "s_done": 0, "wr": {}}
        prep_env = {}

        def emit_prep_steps(k):
            for _ in range(k):
                i = prep_state["i"]
                if i >= len(prep_jobs):
                    return
                prep_state["i"] = i + 1
                tag, idx = prep_jobs[i]
                strip = prep_env["strip"]
                if tag == "wo":
                    src = d_wo[idx * 128:(idx + 1) * 128, :]
                    blks = [(d_swo_hi[idx, b], d_swo_lo[idx, b]) for b in range(4)]
                elif tag == "gw":
                    src = d_gw[idx * 128:(idx + 1) * 128, :]
                    blks = [(d_sgw_hi[idx, b], d_sgw_lo[idx, b]) for b in range(4)]
                elif tag == "uw":
                    src = d_uw[idx * 128:(idx + 1) * 128, :]
                    blks = [(d_suw_hi[idx, b], d_suw_lo[idx, b]) for b in range(4)]
                else:
                    s_i, c4 = idx // 4, idx % 4
                    src = d_dw[s_i * 128:(s_i + 1) * 128,
                               c4 * 2048:(c4 + 1) * 2048]
                    blks = [(d_sdw_hi[s_i, c4 * 4 + b], d_sdw_lo[s_i, c4 * 4 + b])
                            for b in range(4)]
                lw = list(qws["w"]) + prep_env["ld_extra"]()
                ld = dma_in(rPREP, strip[:], src, lw, q="gpsimd")
                shi = 0
                for b in range(4):
                    sap = strip[:, b * 512:(b + 1) * 512]
                    if tag == "wo":
                        shi = onS(lambda e, a=sap: e.copy(out=q_hi[:], in_=a),
                                  [(rPREP, ld), (sV, sV.n)] + list(qws["w"]))
                        vlo = onV(lambda e, a=sap: e.tensor_tensor(
                            out=q_lo[:], in0=a, in1=q_hi[:], op=Alu.subtract),
                            [(sS, shi)])
                    else:
                        shi, vlo = emit_quant(sap, prep_env["ws"], [(rPREP, ld)])
                    dma_wr(wQ, blks[b][0][:, :], q_hi[:], [(sS, shi)])
                    w2 = dma_wr(wQ, blks[b][1][:, :], q_lo[:], [(sV, vlo)])
                    qws["w"] = [(wQ, w2)]
                prep_state["s_done"] = shi
                prep_state["wr"][(tag, idx)] = w2

        rWB = stream("rWB")
        wWB = stream("wWB")
        rXT = stream("rXT")
        wXQT = stream("wXQT")

        with contextlib.ExitStack() as ph_aa:
            xn_fm_hi = sb("xn_fm_hi", [128, 16 * 2048], BF16, ph_aa)
            xn_fm_lo = sb("xn_fm_lo", [128, 16 * 2048], BF16, ph_aa)
            fm_hi_r = xn_fm_hi[:].rearrange("p (c t) -> p c t", t=2048)
            fm_lo_r = xn_fm_lo[:].rearrange("p (c t) -> p c t", t=2048)

            with contextlib.ExitStack() as ph_a0:
                xt = sb("xt", [128, 2048], F32, ph_a0)
                xn = sb("xn", [128, 2048], F32, ph_a0)
                xnhi = sb("xnhi", [128, 2048], BF16, ph_a0)
                xnlo = sb("xnlo", [128, 2048], BF16, ph_a0)
                xqs_hi = sb("xqs_hi", [128, 512], BF16, ph_a0)
                xqs_lo = sb("xqs_lo", [128, 512], BF16, ph_a0)
                wb_stg = sb("wb_stg", [128, 3072], F32, ph_a0)
                a0_t = sb("a0_t", [128, QL], F32, ph_a0)
                a0_u = sb("a0_u", [128, QL], F32, ph_a0)
                a0_f = sb("a0_f", [128, QL], F32, ph_a0)
                a0_strip = sb("a0_strip", [128, 2048], F32, ph_a0)
                lnsum = sb("lnsum", [128, 1], F32, ph_a0)
                lnm = sb("lnm", [128, 1], F32, ph_a0)
                lnnm = sb("lnnm", [128, 1], F32, ph_a0)
                lnvs = sb("lnvs", [128, 1], F32, ph_a0)
                lnsd = sb("lnsd", [128, 1], F32, ph_a0)
                lnr = sb("lnr", [128, 1], F32, ph_a0)
                lnb = sb("lnb", [128, 1], F32, ph_a0)

                prep_env["strip"] = a0_strip
                prep_env["ws"] = {"t": a0_t[:], "u": a0_u[:], "f": a0_f[:]}
                prep_env["ld_extra"] = lambda: []

                tp_free = [0, 0, 0, 0]
                tp_ctr = [0]
                xqs_wr = [0]
                xt_free = {"v": 0, "s": 0}
                xn_free_v = 0
                xnsplit_t = 0

                def tp_batch4(src_tiles, evac_fn, gate, ev_extra=()):
                    bk = tp_ctr[0] % 4
                    tp_ctr[0] += 1
                    psb = P_A[:, bk * 512:(bk + 1) * 512].bitcast(BF16)
                    w0 = list(gate) + [(sS, sI), (sV, tp_free[bk])]
                    tmm = 0
                    for k4, tile in enumerate(src_tiles):
                        tmm = onT(lambda e, t_=tile, k4=k4, psb=psb:
                                  e.transpose(
                            out=psb[:, k4 * 128:(k4 + 1) * 128],
                            in_=t_, identity=ident[:]), w0)
                        w0 = ()
                    ev = onV(lambda e, psb=psb, f=evac_fn: f(
                        e, psb[:, 0:512]), [(sT, tmm)] + list(ev_extra))
                    tp_free[bk] = ev
                    return tmm, ev

                for tt in range(NTT):
                    ld = dma_in(rXT, xt[:], d_x[tt * 128:(tt + 1) * 128, :],
                                [(sV, xt_free["v"]), (sS, xt_free["s"])])
                    for half in range(2):
                        lwb = dma_in(rWB, wb_stg[:],
                                     d_wqkv[tt * 128:(tt + 1) * 128,
                                            half * 3072:(half + 1) * 3072],
                                     [(wWB, wWB.n)], q="gpsimd")
                        for k in range(24):
                            fi = half * 24 + k
                            dma_wr(wWB, d_wqkvB[fi, tt, :, :],
                                   wb_stg[:, k * 128:(k + 1) * 128],
                                   [(rWB, lwb)] if k == 0 else ())
                    onV(lambda e: e.reduce_sum(out=lnsum[:], in_=xt[:],
                                               axis=AX.X), [(rXT, ld)])
                    onV(lambda e: e.tensor_scalar(out=lnm[:], in0=lnsum[:],
                                                  scalar1=INV_H, scalar2=None,
                                                  op0=Alu.mult))
                    vnm = onV(lambda e: e.tensor_scalar(
                        out=lnnm[:], in0=lnm[:], scalar1=-1.0, scalar2=None,
                        op0=Alu.mult))
                    ssq = onS(lambda e: e.activation(
                        out=P_B[:, 0:2048], in_=xt[:], func=Act.Square,
                        bias=lnnm[:], accum_out=lnvs[:]),
                        [(sV, vnm), (rXT, ld)])
                    vv1 = onV(lambda e: e.tensor_scalar(
                        out=lnvs[:], in0=lnvs[:], scalar1=INV_H, scalar2=EPS_LN,
                        op0=Alu.mult, op1=Alu.add), [(sS, ssq)])
                    ssd = onS(lambda e: e.activation(out=lnsd[:], in_=lnvs[:],
                                                     func=Act.Sqrt), [(sV, vv1)])
                    onV(lambda e: e.tensor_scalar(out=q_sp[:], in0=lnsd[:],
                                                  scalar1=1.0, scalar2=None,
                                                  op0=Alu.mult), [(sS, ssd)])
                    vr = onV(lambda e: e.reciprocal(out=lnr[:], in_=lnsd[:]))
                    vb = onV(lambda e: e.tensor_tensor(
                        out=lnb[:], in0=lnnm[:], in1=lnr[:], op=Alu.mult),
                        [(sV, vr)])
                    sxn = onS(lambda e: e.activation(
                        out=xn[:], in_=xt[:], func=Act.Identity, scale=lnr[:],
                        bias=lnb[:]), [(sV, vb), (sV, xn_free_v)])
                    sxh = onS(lambda e: e.copy(out=xnhi[:], in_=xn[:]),
                              [(sT, xnsplit_t)])
                    vxl = onV(lambda e: e.tensor_tensor(
                        out=xnlo[:], in0=xn[:], in1=xnhi[:], op=Alu.subtract),
                        [(sS, sxh), (sT, xnsplit_t)])
                    t_last = 0
                    for b4 in range(4):
                        hcs = [b4 * 4 + k for k in range(4)]
                        t_last, _ = tp_batch4(
                            [xnhi[:, hc * 128:(hc + 1) * 128] for hc in hcs],
                            lambda e, ps, b4=b4, tt=tt: e.tensor_copy(
                                out=fm_hi_r[:, b4 * 4:(b4 + 1) * 4,
                                            tt * 128:(tt + 1) * 128],
                                in_=ps.rearrange("p (c t) -> p c t", t=128)),
                            [(sV, vxl)])
                        t_last, _ = tp_batch4(
                            [xnlo[:, hc * 128:(hc + 1) * 128] for hc in hcs],
                            lambda e, ps, b4=b4, tt=tt: e.tensor_copy(
                                out=fm_lo_r[:, b4 * 4:(b4 + 1) * 4,
                                            tt * 128:(tt + 1) * 128],
                                in_=ps.rearrange("p (c t) -> p c t", t=128)),
                            [(sV, vxl)])
                    xnsplit_t = t_last
                    last_shi, last_vlo = 0, 0
                    for hf in range(4):
                        shi, vlo = emit_quant(xn[:, hf * 512:(hf + 1) * 512],
                                              prep_env["ws"], [(sS, sxn)])
                        qtm, evh = tp_batch4(
                            [q_hi[:, k * 128:(k + 1) * 128] for k in range(4)],
                            lambda e, ps: e.tensor_copy(out=xqs_hi[:], in_=ps),
                            [(sS, shi)], ev_extra=[(wXQT, xqs_wr[0])])
                        qtm, evl = tp_batch4(
                            [q_lo[:, k * 128:(k + 1) * 128] for k in range(4)],
                            lambda e, ps: e.tensor_copy(out=xqs_lo[:], in_=ps),
                            [(sV, vlo)], ev_extra=[(wXQT, xqs_wr[0])])
                        qws["w"] = [(sT, qtm)]
                        dma_wr(wXQT, d_xqT_hi[tt, hf, :, :], xqs_hi[:],
                               [(sV, evh)])
                        xqs_wr[0] = dma_wr(wXQT, d_xqT_lo[tt, hf, :, :],
                                           xqs_lo[:], [(sV, evl)])
                        last_shi, last_vlo = shi, vlo
                    emit_prep_steps(1)
                    xt_free = {"v": vxl, "s": last_shi}
                    xn_free_v = last_vlo

            barrier()

            with contextlib.ExitStack() as ph_a:
                wsl = sb("wsl", [128, 2048], F32, ph_a)
                whi = sb("whi", [128, 2048], BF16, ph_a)
                wlo = sb("wlo", [128, 2048], BF16, ph_a)
                Qhi = sb("Qhi", [128, 2048], BF16, ph_a)
                Qlo = sb("Qlo", [128, 2048], BF16, ph_a)
                Khi = sb("Khi", [128, 2048], BF16, ph_a)
                Klo = sb("Klo", [128, 2048], BF16, ph_a)
                Vfm = sb("Vfm", [128, 2048], F32, ph_a)
                E_t = [sb(f"E{i}", [128, 2048], F32, ph_a) for i in range(2)]
                valhi = sb("valhi", [128, 512], BF16, ph_a)
                vallo = sb("vallo", [128, 512], BF16, ph_a)
                rw4 = sb("rw4", [128, 8], F32, ph_a)
                rws = sb("rws", [128, 2], F32, ph_a)
                r_t = sb("r_t", [128, 2], F32, ph_a)

                c_row = E_t[0][0:1, :]
                prep_env["strip"] = E_t[0]
                prep_env["ws"] = {"t": Vfm[:, 0:512], "u": Vfm[:, 512:1024],
                                  "f": Vfm[:, 1024:1536]}
                prep_env["ld_extra"] = lambda: [(sT, sT.n)]

                rWSL = stream("rWSL")
                wVAL = stream("wVAL")
                wsl_free = {"v": 0, "s": 0}
                w_mm_free = 0
                qkv_ps = [(0, 0), (0, 0)]
                sc_ps = [(0, 0), (0, 0)]
                E_rd = [0, 0]
                rw_free = [0, 0]
                rt_busy = [0, 0]
                rt_ready = [0, 0]
                val_state = {"s": 0, "wr": 0}

                for hd in range(NH):
                    qk_gate = (0, 0)
                    for comp in range(3):
                        fi = comp * 16 + hd
                        lw0 = [(sS, wsl_free["s"]), (sV, wsl_free["v"])]
                        for hc in range(16):
                            dma_in(rWSL, wsl[:, hc * 128:(hc + 1) * 128],
                                   d_wqkvB[fi, hc, :, :], lw0)
                            lw0 = ()
                        wld = rWSL.n
                        if comp == 0:
                            vsc = 0
                            for p4 in range(4):
                                onV(lambda e, p4=p4: e.tensor_scalar(
                                    out=q_v[:],
                                    in0=wsl[:, p4 * 512:(p4 + 1) * 512],
                                    scalar1=CW_LO, scalar2=None, op0=Alu.mult),
                                    [(rWSL, wld)] if p4 == 0 else ())
                                vsc = onV(lambda e, p4=p4: e.scalar_tensor_tensor(
                                    out=wsl[:, p4 * 512:(p4 + 1) * 512],
                                    in0=wsl[:, p4 * 512:(p4 + 1) * 512],
                                    scalar=CW_HI, in1=q_v[:],
                                    op0=Alu.mult, op1=Alu.add))
                            wait_w = [(sV, vsc)]
                        else:
                            wait_w = [(rWSL, wld)]
                        swh = onS(lambda e: e.copy(out=whi[:], in_=wsl[:]),
                                  wait_w + [(sT, w_mm_free)])
                        vwl = onV(lambda e: e.tensor_tensor(
                            out=wlo[:], in0=wsl[:], in1=whi[:], op=Alu.subtract),
                            [(sS, swh), (sT, w_mm_free)])
                        wsl_free = {"v": vwl, "s": swh}
                        for tc in range(4):
                            qb = (comp * 4 + tc) % 2
                            wq = [(sV, vwl), (sV, qkv_ps[qb][0]),
                                  (sS, qkv_ps[qb][1])]
                            mm = 0
                            for hc in range(16):
                                for pi, (wt, xf) in enumerate(
                                        ((whi, fm_hi_r), (whi, fm_lo_r),
                                         (wlo, fm_hi_r))):
                                    mm = onT(lambda e, wt=wt, xf=xf, hc=hc,
                                             tc=tc, qb=qb,
                                             st=(hc == 0 and pi == 0),
                                             sp=(hc == 15 and pi == 2):
                                             e.matmul(
                                        out=P_A[:, qb * 512:(qb + 1) * 512],
                                        lhsT=wt[:, hc * 128:(hc + 1) * 128],
                                        rhs=xf[:, hc, tc * 512:(tc + 1) * 512],
                                        start=st, stop=sp), wq)
                                    wq = ()
                            w_mm_free = mm
                            if comp < 2:
                                dsth, dstl = ((Qhi, Qlo) if comp == 0
                                              else (Khi, Klo))
                                sh = onS(lambda e, d=dsth, tc=tc, qb=qb: e.copy(
                                    out=d[:, tc * 512:(tc + 1) * 512],
                                    in_=P_A[:, qb * 512:(qb + 1) * 512]),
                                    [(sT, mm)])
                                vl = onV(lambda e, dh=dsth, dl=dstl, tc=tc,
                                         qb=qb: e.tensor_tensor(
                                             out=dl[:, tc * 512:(tc + 1) * 512],
                                             in0=P_A[:, qb * 512:(qb + 1) * 512],
                                             in1=dh[:, tc * 512:(tc + 1) * 512],
                                             op=Alu.subtract), [(sS, sh)])
                                qkv_ps[qb] = (vl, sh)
                                if comp == 1:
                                    qk_gate = (vl, sh)
                            else:
                                vl = onV(lambda e, tc=tc, qb=qb: e.tensor_copy(
                                    out=Vfm[:, tc * 512:(tc + 1) * 512],
                                    in_=P_A[:, qb * 512:(qb + 1) * 512]),
                                    [(sT, mm), (sS, prep_state["s_done"])])
                                qkv_ps[qb] = (vl, qkv_ps[qb][1])
                        if comp < 2:
                            emit_prep_steps(2)
                    col_mm = [0]

                    def emit_colsum(qtc):
                        ebc = qtc % 2
                        w0 = [(sV, rt_ready[ebc])]
                        mmc = 0
                        for kc in range(4):
                            mmc = onT(lambda e, ebc=ebc, kc=kc, qtc=qtc:
                                      e.matmul(
                                out=P_B[0:1, kc * 512:(kc + 1) * 512],
                                lhsT=r_t[:, ebc:ebc + 1],
                                rhs=E_t[ebc][:, kc * 512:(kc + 1) * 512],
                                start=(qtc == 0), stop=(qtc == 15)), w0)
                            w0 = ()
                        rt_busy[ebc] = mmc
                        E_rd[ebc] = mmc
                        col_mm[0] = mmc

                    se = 0
                    for qt in range(16):
                        eb = qt % 2
                        for kc in range(4):
                            b2 = kc % 2
                            w0 = [(sS, sc_ps[b2][1]), (sV, sc_ps[b2][0])]
                            if qt == 0 and kc == 0:
                                w0 += [(sV, qk_gate[0]), (sS, qk_gate[1])]
                            mm = 0
                            for pi, (qq, kk) in enumerate(
                                    ((Qhi, Khi), (Qhi, Klo), (Qlo, Khi))):
                                mm = onT(lambda e, qq=qq, kk=kk, qt=qt, kc=kc,
                                         b2=b2, st=(pi == 0), sp=(pi == 2):
                                         e.matmul(
                                    out=P_A[:, (2 + b2) * 512:(3 + b2) * 512],
                                    lhsT=qq[:, qt * 128:(qt + 1) * 128],
                                    rhs=kk[:, kc * 512:(kc + 1) * 512],
                                    start=st, stop=sp), w0)
                                w0 = ()
                            we = [(sT, mm)]
                            if kc == 0:
                                we += [(sV, rw_free[eb]), (sT, E_rd[eb])]
                                if qt == 0:
                                    we.append((rPREP, rPREP.n))
                            se = onS(lambda e, eb=eb, kc=kc, b2=b2:
                                     e.activation(
                                out=E_t[eb][:, kc * 512:(kc + 1) * 512],
                                in_=P_A[:, (2 + b2) * 512:(3 + b2) * 512],
                                func=Act.Exp, bias=cst_t[:, C_NEG8:C_NEG8 + 1],
                                accum_out=rw4[:, eb * 4 + kc:eb * 4 + kc + 1]),
                                we)
                            sc_ps[b2] = (sc_ps[b2][0], se)
                        vr1 = onV(lambda e, eb=eb: e.reduce_sum(
                            out=rws[:, eb:eb + 1],
                            in_=rw4[:, eb * 4:eb * 4 + 4], axis=AX.X),
                            [(sS, se)])
                        rw_free[eb] = vr1
                        onV(lambda e, eb=eb: e.tensor_scalar(
                            out=q_sp[:], in0=rws[:, eb:eb + 1], scalar1=1.0,
                            scalar2=None, op0=Alu.mult))
                        vrr = onV(lambda e, eb=eb: e.reciprocal(
                            out=r_t[:, eb:eb + 1], in_=rws[:, eb:eb + 1]),
                            [(sT, rt_busy[eb])])
                        rt_ready[eb] = vrr
                        if qt > 0:
                            emit_colsum(qt - 1)
                    emit_colsum(15)
                    vc = onV(lambda e: e.tensor_copy(out=c_row,
                                                     in_=P_B[0:1, :]),
                             [(sT, col_mm[0])])
                    crep_last = 0
                    for kc in range(4):
                        b2 = kc % 2
                        crep_mm = onT(lambda e, kc=kc, b2=b2: e.matmul(
                            out=P_A[:, (2 + b2) * 512:(3 + b2) * 512],
                            lhsT=ones_t[:],
                            rhs=c_row[0:1, kc * 512:(kc + 1) * 512],
                            start=True, stop=True),
                            [(sV, vc), (sS, sc_ps[b2][1]), (sV, sc_ps[b2][0])])
                        crep_last = crep_mm
                        vv = onV(lambda e, kc=kc, b2=b2: e.tensor_tensor(
                            out=q_ya[:], in0=Vfm[:, kc * 512:(kc + 1) * 512],
                            in1=P_A[:, (2 + b2) * 512:(3 + b2) * 512],
                            op=Alu.mult),
                            [(sT, crep_mm), (sS, prep_state["s_done"]),
                             (sS, val_state["s"])])
                        svh = onS(lambda e: e.copy(out=valhi[:], in_=q_ya[:]),
                                  [(sV, vv), (wVAL, val_state["wr"])])
                        vvl = onV(lambda e: e.tensor_tensor(
                            out=vallo[:], in0=q_ya[:], in1=valhi[:],
                            op=Alu.subtract), [(sS, svh)])
                        wv = 0
                        for tq in range(4):
                            ti = kc * 4 + tq
                            dma_wr(wVAL, d_sval_hi[ti, hd, :, :],
                                   valhi[:, tq * 128:(tq + 1) * 128],
                                   [(sS, svh)] if tq == 0 else ())
                            wv = dma_wr(wVAL, d_sval_lo[ti, hd, :, :],
                                        vallo[:, tq * 128:(tq + 1) * 128],
                                        [(sV, vvl)] if tq == 0 else ())
                        val_state = {"s": svh, "wr": wv}
                        sc_ps[b2] = (vv, sc_ps[b2][1])
                    E_rd[0] = crep_last
                    emit_prep_steps(4)
                if prep_state["i"] < 144:
                    emit_prep_steps(144 - prep_state["i"])

        barrier()

        with contextlib.ExitStack() as ph_b:
            wo_hi = sb("wo_hi", [128, 16 * 1024], BF16, ph_b)
            wo_lo = sb("wo_lo", [128, 16 * 1024], BF16, ph_b)
            vt_hi = [sb(f"vt_hi{i}", [128, 2048], BF16, ph_b) for i in range(2)]
            vt_lo = [sb(f"vt_lo{i}", [128, 2048], BF16, ph_b) for i in range(2)]
            ao = sb("ao", [128, 1024], F32, ph_b)
            ml_in = [sb(f"ml_in{i}", [128, 2048], BF16, ph_b) for i in range(2)]
            ml_st = [sb(f"ml_st{i}", [128, 2048], BF16, ph_b) for i in range(2)]

            rWO = stream("rWO")
            rVT = [stream("rVT0"), stream("rVT1")]
            wATTN = stream("wATTN")
            rMILL = [stream("rMILL0"), stream("rMILL1")]
            wMILL = [stream("wMILL0"), stream("wMILL1")]

            mill_jobs = [(srcb, dstb, s)
                         for (srcb, dstb) in ((d_sgw_hi, d_gwT_hi),
                                              (d_sgw_lo, d_gwT_lo),
                                              (d_suw_hi, d_uwT_hi),
                                              (d_suw_lo, d_uwT_lo))
                         for s in range(64)]
            mill_state = {"i": 0, "in_free": [0, 0], "st_free": [0, 0],
                          "ps_free": [0, 0]}

            def emit_mill_steps(k):
                for _ in range(k):
                    j = mill_state["i"]
                    if j >= len(mill_jobs):
                        return
                    mill_state["i"] = j + 1
                    srcb, dstb, s = mill_jobs[j]
                    sl = j % 2
                    ld = 0
                    lw = [(sT, mill_state["in_free"][sl])]
                    for c in range(4):
                        ld = dma_in(rMILL[sl],
                                    ml_in[sl][:, c * 512:(c + 1) * 512],
                                    srcb[s, c, :, :], lw)
                        lw = ()
                    tmm = 0
                    for b4 in range(4):
                        pb = b4 % 2
                        psb = P_B[:, pb * 512:(pb + 1) * 512].bitcast(BF16)
                        w0 = [(rMILL[sl], ld), (sS, sI),
                              (sV, mill_state["ps_free"][pb])]
                        for k4 in range(4):
                            hc = b4 * 4 + k4
                            tmm = onT(lambda e, hc=hc, k4=k4, psb=psb, sl=sl:
                                      e.transpose(
                                out=psb[:, k4 * 128:(k4 + 1) * 128],
                                in_=ml_in[sl][:, hc * 128:(hc + 1) * 128],
                                identity=ident[:]), w0)
                            w0 = ()
                        ev = onV(lambda e, b4=b4, psb=psb, sl=sl: e.tensor_copy(
                            out=ml_st[sl][:, b4 * 512:(b4 + 1) * 512],
                            in_=psb[:, 0:512]),
                            [(sT, tmm), (wMILL[sl], mill_state["st_free"][sl])])
                        mill_state["ps_free"][pb] = ev
                    mill_state["in_free"][sl] = tmm
                    wv = dma_wr(wMILL[sl], dstb[s, :, :], ml_st[sl][:],
                                [(sV, mill_state["ps_free"][0]),
                                 (sV, mill_state["ps_free"][1])])
                    mill_state["st_free"][sl] = wv

            wo_free = 0
            po_free = [0, 0]
            wr_attn = 0
            mm_l = 0
            for hh in range(2):
                lw = [(sT, wo_free)]
                for fc in range(16):
                    dma_in(rWO, wo_hi[:, fc * 1024:fc * 1024 + 512],
                           d_swo_hi[fc, hh * 2, :, :], lw)
                    lw = ()
                    dma_in(rWO, wo_hi[:, fc * 1024 + 512:(fc + 1) * 1024],
                           d_swo_hi[fc, hh * 2 + 1, :, :])
                    dma_in(rWO, wo_lo[:, fc * 1024:fc * 1024 + 512],
                           d_swo_lo[fc, hh * 2, :, :])
                    dma_in(rWO, wo_lo[:, fc * 1024 + 512:(fc + 1) * 1024],
                           d_swo_lo[fc, hh * 2 + 1, :, :])
                wo_ready = rWO.n
                vt_free = [0, 0]
                for tt in range(NTT):
                    vb = tt % 2
                    lw = [(sT, vt_free[vb])]
                    for f in range(16):
                        dma_in(rVT[vb], vt_hi[vb][:, f * 128:(f + 1) * 128],
                               d_sval_hi[tt, f, :, :], lw)
                        lw = ()
                        dma_in(rVT[vb], vt_lo[vb][:, f * 128:(f + 1) * 128],
                               d_sval_lo[tt, f, :, :])
                    vready = rVT[vb].n
                    for hcB in range(2):
                        pb = hcB
                        wq = [(rVT[vb], vready), (rWO, wo_ready),
                              (sV, po_free[pb])]
                        for fc in range(16):
                            for pi, (vv_, ww_) in enumerate(
                                    ((vt_hi[vb], wo_hi), (vt_hi[vb], wo_lo),
                                     (vt_lo[vb], wo_hi))):
                                mm_l = onT(lambda e, vv_=vv_, ww_=ww_, fc=fc,
                                           hcB=hcB, pb=pb,
                                           st=(fc == 0 and pi == 0),
                                           sp=(fc == 15 and pi == 2): e.matmul(
                                    out=P_A[:, pb * 512:(pb + 1) * 512],
                                    lhsT=vv_[:, fc * 128:(fc + 1) * 128],
                                    rhs=ww_[:, fc * 1024 + hcB * 512:
                                            fc * 1024 + (hcB + 1) * 512],
                                    start=st, stop=sp), wq)
                                wq = ()
                        pv = onV(lambda e, hcB=hcB, pb=pb: e.tensor_copy(
                            out=ao[:, hcB * 512:(hcB + 1) * 512],
                            in_=P_A[:, pb * 512:(pb + 1) * 512]),
                            [(sT, mm_l), (wATTN, wr_attn)])
                        po_free[pb] = pv
                    vt_free[vb] = mm_l
                    dma_wr(wATTN, d_sattnB[tt, hh * 2, :, :], ao[:, 0:512],
                           [(sV, po_free[0]), (sV, po_free[1])])
                    wr_attn = dma_wr(wATTN, d_sattnB[tt, hh * 2 + 1, :, :],
                                     ao[:, 512:1024], ())
                    emit_mill_steps(8)
                wo_free = mm_l
            emit_mill_steps(len(mill_jobs))

        barrier()

        with contextlib.ExitStack() as ph_c:
            xqg_hi = sb("xqg_hi", [128, 8192], BF16, ph_c)
            xqg_lo = sb("xqg_lo", [128, 8192], BF16, ph_c)
            ws_w = [[sb(f"wsC{i}_{j}", [128, 2048], BF16, ph_c)
                     for j in range(4)] for i in range(2)]
            hT = [sb(f"hT{i}", [128, 512], F32, ph_c) for i in range(2)]
            hstage = [[sb(f"hstg{g}_{i}", [128, 512], F32, ph_c)
                       for i in range(4)] for g in range(2)]
            hqs_hi = sb("hqs_hi", [128, 512], BF16, ph_c)
            hqs_lo = sb("hqs_lo", [128, 512], BF16, ph_c)
            gtmp = sb("gtmp", [128, 512], F32, ph_c)
            dml_in = [sb(f"dml_in{i}", [128, 2048], BF16, ph_c) for i in range(2)]
            dml_st = [sb(f"dml_st{i}", [128, 2048], BF16, ph_c) for i in range(2)]
            c_t = sb("c_t", [128, QL], F32, ph_c)
            c_u = sb("c_u", [128, QL], F32, ph_c)
            c_f = sb("c_f", [128, QL], F32, ph_c)
            c_strip = sb("c_strip", [128, 2048], F32, ph_c)

            prep_env["strip"] = c_strip
            prep_env["ws"] = {"t": c_t[:], "u": c_u[:], "f": c_f[:]}
            prep_env["ld_extra"] = lambda: []

            rXQG = stream("rXQG")
            rGWS = [stream("rGWS0"), stream("rGWS1")]
            rDML = [stream("rDML0"), stream("rDML1")]
            wDML = [stream("wDML0"), stream("wDML1")]
            wHQT = stream("wHQT")

            xqg_w_hi = xqg_hi[:].rearrange("p (a c4 q) -> p a c4 q", a=4, c4=4)
            xqg_w_lo = xqg_lo[:].rearrange("p (a c4 q) -> p a c4 q", a=4, c4=4)

            dml_jobs = [(srcb, dstb, idx)
                        for (srcb, dstb) in ((d_sdw_hi, d_dwT_hi),
                                             (d_sdw_lo, d_dwT_lo))
                        for idx in range(64)]
            dml_state = {"i": 0, "in_free": [0, 0], "st_free": [0, 0],
                         "ps_free": [0, 0]}

            def emit_dml_steps(k):
                for _ in range(k):
                    j = dml_state["i"]
                    if j >= len(dml_jobs):
                        return
                    dml_state["i"] = j + 1
                    srcb, dstb, idx = dml_jobs[j]
                    s_i, c4 = idx // 4, idx % 4
                    sl = j % 2
                    ld = 0
                    lw = [(sT, dml_state["in_free"][sl]),
                          (wQ, prep_state["wr"][("dw", idx)])]
                    for c in range(4):
                        ld = dma_in(rDML[sl],
                                    dml_in[sl][:, c * 512:(c + 1) * 512],
                                    srcb[s_i, c4 * 4 + c, :, :], lw)
                        lw = ()
                    tmm = 0
                    for b4 in range(4):
                        pb = b4 % 2
                        psb = P_B[:, pb * 512:(pb + 1) * 512].bitcast(BF16)
                        w0 = [(rDML[sl], ld), (sS, sI),
                              (sV, dml_state["ps_free"][pb])]
                        for k4 in range(4):
                            icc = b4 * 4 + k4
                            tmm = onT(lambda e, icc=icc, k4=k4, psb=psb, sl=sl:
                                      e.transpose(
                                out=psb[:, k4 * 128:(k4 + 1) * 128],
                                in_=dml_in[sl][:, icc * 128:(icc + 1) * 128],
                                identity=ident[:]), w0)
                            w0 = ()
                        ev = onV(lambda e, b4=b4, psb=psb, sl=sl: e.tensor_copy(
                            out=dml_st[sl][:, b4 * 512:(b4 + 1) * 512],
                            in_=psb[:, 0:512]),
                            [(sT, tmm), (wDML[sl], dml_state["st_free"][sl])])
                        dml_state["ps_free"][pb] = ev
                    dml_state["in_free"][sl] = tmm
                    wv = 0
                    evw = [(sV, dml_state["ps_free"][0]),
                           (sV, dml_state["ps_free"][1])]
                    for icc in range(16):
                        wv = dma_wr(wDML[sl], dstb[c4 * 16 + icc, s_i, :, :],
                                    dml_st[sl][:, icc * 128:(icc + 1) * 128],
                                    evw)
                        evw = ()
                    dml_state["st_free"][sl] = wv

            wch_free = [0, 0]
            gate_free = [0, 0]
            up_free = [0, 0]
            vh_last = 0
            hq_ps_free = [0, 0]
            hq_ps_ctr = [0]
            hT_free = [0, 0]
            hst_gate = [[0] * 4 for _ in range(2)]
            hst_ev = [[0] * 4 for _ in range(2)]
            hqs_wr = [0]
            grp_free = 0
            iter_ctr = 0
            strip_seq = [0]
            ready = []
            pend = []

            def hq_tp4(tiles, idap, gate, bf=True):
                bk = 2 + hq_ps_ctr[0] % 2
                hq_ps_ctr[0] += 1
                reg = P_B[:, bk * 512:(bk + 1) * 512]
                psv = reg.bitcast(BF16) if bf else reg
                w0 = list(gate) + [(sS, sI), (sV, hq_ps_free[bk - 2])]
                tmm = 0
                for k4, tile in enumerate(tiles):
                    tmm = onT(lambda e, t_=tile, k4=k4, psv=psv: e.transpose(
                        out=psv[:, k4 * 128:(k4 + 1) * 128],
                        in_=t_, identity=idap), w0)
                    w0 = ()
                return tmm, psv

            def flush_tp(ent):
                g4d, icd, tt4d, shi, vlo, _ = ent
                qtm, psv = hq_tp4(
                    [q_hi[:, k * 128:(k + 1) * 128] for k in range(4)],
                    ident[:], [(sS, shi), (wHQT, hqs_wr[0])])
                evh = onV(lambda e, psv=psv: e.tensor_copy(
                    out=hqs_hi[:], in_=psv[:, 0:512]), [(sT, qtm)])
                hq_ps_free[(hq_ps_ctr[0] - 1) % 2] = evh
                qtm, psv = hq_tp4(
                    [q_lo[:, k * 128:(k + 1) * 128] for k in range(4)],
                    ident[:], [(sV, vlo), (wHQT, hqs_wr[0])])
                evl = onV(lambda e, psv=psv: e.tensor_copy(
                    out=hqs_lo[:], in_=psv[:, 0:512]), [(sT, qtm)])
                hq_ps_free[(hq_ps_ctr[0] - 1) % 2] = evl
                qws["w"] = [(sT, qtm)]
                dma_wr(wHQT, d_hqT_hi[g4d * 4 + tt4d, icd, :, :],
                       hqs_hi[:], [(sV, evh)])
                hqs_wr[0] = dma_wr(wHQT, d_hqT_lo[g4d * 4 + tt4d, icd, :, :],
                                   hqs_lo[:], [(sV, evl)])

            def flush_pend():
                while pend and pend[0][5] < strip_seq[0]:
                    flush_tp(pend.pop(0))

            def emit_one_chain():
                if ready:
                    g4d, icd, tt4d, buf = ready.pop(0)
                    shi, vlo = emit_quant(hstage[buf][tt4d][:],
                                          prep_env["ws"],
                                          [(sV, hst_ev[buf][tt4d])])
                    hst_gate[buf][tt4d] = shi
                    pend.append((g4d, icd, tt4d, shi, vlo, strip_seq[0]))
                strip_seq[0] += 1

            for g4 in range(4):
                lw = [(sT, grp_free)]
                for tt4 in range(4):
                    for hf in range(4):
                        dma_in(rXQG,
                               xqg_w_hi[:, hf, :, tt4 * 128:(tt4 + 1) * 128],
                               d_xqT_hi[g4 * 4 + tt4, hf, :, :].rearrange(
                                   "p (c t) -> p c t", t=128), lw)
                        lw = ()
                        dma_in(rXQG,
                               xqg_w_lo[:, hf, :, tt4 * 128:(tt4 + 1) * 128],
                               d_xqT_lo[g4 * 4 + tt4, hf, :, :].rearrange(
                                   "p (c t) -> p c t", t=128))
                xq_ready = rXQG.n
                mm_l = 0
                for istrip in range(64):
                    pb = istrip % 2
                    lw = [(sT, wch_free[pb])]
                    for j, dsp in enumerate((d_gwT_hi[istrip], d_gwT_lo[istrip],
                                             d_uwT_hi[istrip],
                                             d_uwT_lo[istrip])):
                        dma_in(rGWS[pb], ws_w[pb][j][:], dsp, lw)
                        lw = ()
                    wready = rGWS[pb].n
                    for half, (w_h, w_l) in enumerate(
                            ((ws_w[pb][0], ws_w[pb][1]),
                             (ws_w[pb][2], ws_w[pb][3]))):
                        bk = pb if half == 0 else 2 + pb
                        wq = [(rGWS[pb], wready), (rXQG, xq_ready)]
                        wq.append((sS, gate_free[pb]) if half == 0
                                  else (sV, up_free[pb]))
                        for hc in range(16):
                            for pi, (wt, xf) in enumerate(
                                    ((w_h, xqg_hi), (w_h, xqg_lo),
                                     (w_l, xqg_hi))):
                                mm_l = onT(lambda e, wt=wt, xf=xf, hc=hc,
                                           bk=bk, st=(hc == 0 and pi == 0),
                                           sp=(hc == 15 and pi == 2): e.matmul(
                                    out=P_A[:, bk * 512:(bk + 1) * 512],
                                    lhsT=wt[:, hc * 128:(hc + 1) * 128],
                                    rhs=xf[:, hc * 512:(hc + 1) * 512],
                                    start=st, stop=sp), wq)
                                wq = ()
                    wch_free[pb] = mm_l
                    sl = istrip % 2
                    sg = onS(lambda e, pb=pb: e.activation(
                        out=gtmp[:], in_=P_A[:, pb * 512:(pb + 1) * 512],
                        func=Act.Silu), [(sT, mm_l), (sV, vh_last)])
                    gate_free[pb] = sg
                    vh = onV(lambda e, pb=pb, sl=sl: e.tensor_tensor(
                        out=hT[sl][:], in0=gtmp[:],
                        in1=P_A[:, (2 + pb) * 512:(3 + pb) * 512],
                        op=Alu.mult), [(sS, sg), (sT, hT_free[sl])])
                    up_free[pb] = vh
                    vh_last = vh
                    i4 = istrip % 4
                    ic = istrip // 4
                    buf = ic % 2
                    tmm = 0
                    ev = 0
                    for tt4 in range(4):
                        tmm, psv = hq_tp4(
                            [hT[sl][:, tt4 * 128:(tt4 + 1) * 128]],
                            identF[:], [(sV, vh)], bf=False)
                        ev = onV(lambda e, tt4=tt4, i4=i4, buf=buf, psv=psv:
                                 e.tensor_copy(
                            out=hstage[buf][tt4][:, i4 * 128:(i4 + 1) * 128],
                            in_=psv[:, 0:128]),
                            [(sT, tmm)]
                            + ([(sS, hst_gate[buf][tt4])] if i4 == 0 else []))
                        hq_ps_free[(hq_ps_ctr[0] - 1) % 2] = ev
                        hst_ev[buf][tt4] = ev
                    hT_free[sl] = tmm
                    if i4 == 3:
                        for tt4 in range(4):
                            ready.append((g4, ic, tt4, buf))
                    flush_pend()
                    if iter_ctr < 64:
                        emit_prep_steps(1)
                    elif iter_ctr < 192:
                        emit_dml_steps(1)
                    iter_ctr += 1
                    emit_one_chain()
                grp_free = mm_l
            while ready or pend:
                flush_pend()
                emit_one_chain()
            emit_prep_steps(len(prep_jobs))
            emit_dml_steps(len(dml_jobs))

        barrier()

        with contextlib.ExitStack() as ph_d:
            dwc_hi = [sb(f"dwch{i}", [128, 2048], BF16, ph_d) for i in range(2)]
            dwc_lo = [sb(f"dwcl{i}", [128, 2048], BF16, ph_d) for i in range(2)]
            hq_h = [sb(f"hqh{i}", [128, 4096], BF16, ph_d) for i in range(2)]
            hq_l = [sb(f"hql{i}", [128, 4096], BF16, ph_d) for i in range(2)]
            gsl = sb("gsl", [128, 512], F32, ph_d)
            osl = sb("osl", [128, 512], F32, ph_d)
            ssl = [sb(f"ssl{i}", [128, 512], F32, ph_d) for i in range(2)]
            xsl = [sb(f"xsl{i}", [128, 512], F32, ph_d) for i in range(2)]

            rDW = [stream("rDW0"), stream("rDW1")]
            rXD = [stream("rXD0"), stream("rXD1")]
            wOUT = stream("wOUT")

            def PPbank(t8):
                PP = P_A if t8 < 4 else P_B
                return PP[:, (t8 % 4) * 512:((t8 % 4) + 1) * 512]

            dw_free = [0, 0]
            bank_sfree = [0] * 8
            out_wr = 0
            vo1_hist = [0, 0]
            vo2_hist = [0, 0]

            def emit_xs_loads(tkg, hg, t8):
                tt = tkg * 8 + t8
                sl = t8 % 2
                lx = dma_in(rXD[sl], xsl[sl][:],
                            d_x[tt * 128:(tt + 1) * 128,
                                hg * 512:(hg + 1) * 512], [(sV, vo2_hist[sl])])
                ls = dma_in(rXD[sl], ssl[sl][:], d_sattnB[tt, hg, :, :],
                            [(sV, vo1_hist[sl])])
                return lx, ls, sl

            for tkg in range(2):
                for hg in range(4):
                    mm_l = 0
                    for ic in range(16):
                        cb = ic % 2
                        lw = [(sT, dw_free[cb])]
                        for isub in range(4):
                            istrip = ic * 4 + isub
                            for ht in range(4):
                                dma_in(rDW[cb],
                                       dwc_hi[cb][:, isub * 512 + ht * 128:
                                                  isub * 512 + (ht + 1) * 128],
                                       d_dwT_hi[istrip, hg * 4 + ht, :, :], lw)
                                lw = ()
                                dma_in(rDW[cb],
                                       dwc_lo[cb][:, isub * 512 + ht * 128:
                                                  isub * 512 + (ht + 1) * 128],
                                       d_dwT_lo[istrip, hg * 4 + ht, :, :])
                        for t8 in range(8):
                            dma_in(rDW[cb],
                                   hq_h[cb][:, t8 * 512:(t8 + 1) * 512],
                                   d_hqT_hi[tkg * 8 + t8, ic, :, :])
                            dma_in(rDW[cb],
                                   hq_l[cb][:, t8 * 512:(t8 + 1) * 512],
                                   d_hqT_lo[tkg * 8 + t8, ic, :, :])
                        dready = rDW[cb].n
                        first = True
                        for t8 in range(8):
                            for isub in range(4):
                                for pi, (hh_, dd_) in enumerate(
                                        ((hq_h[cb], dwc_hi[cb]),
                                         (hq_h[cb], dwc_lo[cb]),
                                         (hq_l[cb], dwc_hi[cb]))):
                                    st_f = (ic == 0 and isub == 0 and pi == 0)
                                    wq = []
                                    if first:
                                        wq.append((rDW[cb], dready))
                                        first = False
                                    if st_f:
                                        wq.append((sS, bank_sfree[t8]))
                                    mm_l = onT(lambda e, hh_=hh_, dd_=dd_,
                                               isub=isub, t8=t8, st=st_f,
                                               sp=(ic == 15 and isub == 3
                                                   and pi == 2): e.matmul(
                                        out=PPbank(t8),
                                        lhsT=hh_[:, t8 * 512 + isub * 128:
                                                 t8 * 512 + (isub + 1) * 128],
                                        rhs=dd_[:, isub * 512:
                                                (isub + 1) * 512],
                                        start=st, stop=sp), wq)
                        dw_free[cb] = mm_l
                    lx, ls, lsl = emit_xs_loads(tkg, hg, 0)
                    for t8 in range(8):
                        tt = tkg * 8 + t8
                        sl = t8 % 2
                        nlx = nls = 0
                        if t8 < 7:
                            nlx, nls, _ = emit_xs_loads(tkg, hg, t8 + 1)
                        sg_d = onS(lambda e, t8=t8: e.activation(
                            out=gsl[:], in_=PPbank(t8), func=Act.Silu),
                            [(sT, mm_l), (sV, vo1_hist[0]),
                             (sV, vo1_hist[1])])
                        bank_sfree[t8] = sg_d
                        vo1 = onV(lambda e, sl=sl: e.tensor_tensor(
                            out=osl[:], in0=gsl[:], in1=ssl[sl][:],
                            op=Alu.mult),
                            [(sS, sg_d), (rXD[sl], ls), (wOUT, out_wr)])
                        vo2 = onV(lambda e, sl=sl: e.tensor_tensor(
                            out=osl[:], in0=osl[:], in1=xsl[sl][:],
                            op=Alu.add), [(rXD[sl], lx)])
                        out_wr = dma_wr(wOUT,
                                        d_out[tt * 128:(tt + 1) * 128,
                                              hg * 512:(hg + 1) * 512],
                                        osl[:], [(sV, vo2)])
                        vo1_hist[sl] = vo1
                        vo2_hist[sl] = vo2
                        if t8 < 7:
                            lx, ls = nlx, nls

        P.on("sync", None, [(s_, s_.n) for s_ in _streams.values()])

        with contextlib.ExitStack() as sem_stack:
            for s_ in all_sems:
                s_.h = sem_stack.enter_context(nc.semaphore(s_.name))
            block = sem_stack.enter_context(nc.Block())

            def run_queue(eng_name):
                def body(eng):
                    for (fn, waits, inc, inc_n) in P.q[eng_name]:
                        for (s_, v) in waits:
                            eng.wait_ge(s_.h, v)
                        if fn is not None:
                            ins = fn(eng)
                            if inc is not None:
                                ins.then_inc(inc.h, inc_n)
                        elif inc is not None:
                            eng.nop().then_inc(inc.h, inc_n)
                return body

            block.sync(run_queue("sync"))
            block.vector(run_queue("vector"))
            block.scalar(run_queue("scalar"))
            block.tensor(run_queue("tensor"))
            block.gpsimd(run_queue("gpsimd"))

    return nc


_NC_CACHE = {}


def kernel(L0_attn_x, L0_attn_w_qkv, L0_attn_w_o, L0_ffn_gate_w, L0_ffn_up_w,
           L0_ffn_down_w):
    x = np.ascontiguousarray(np.asarray(L0_attn_x, dtype=np.float32))
    wqkv = np.ascontiguousarray(np.asarray(L0_attn_w_qkv, dtype=np.float32))
    wo = np.ascontiguousarray(np.asarray(L0_attn_w_o, dtype=np.float32))
    gw = np.ascontiguousarray(np.asarray(L0_ffn_gate_w, dtype=np.float32))
    uw = np.ascontiguousarray(np.asarray(L0_ffn_up_w, dtype=np.float32))
    dw = np.ascontiguousarray(np.asarray(L0_ffn_down_w, dtype=np.float32))
    cst = np.tile(np.array([CST_VALS], np.float32), (128, 1))
    ones1 = np.ones((1, 128), np.float32)
    idf = np.eye(128, dtype=np.float32)

    if "nc" not in _NC_CACHE:
        _NC_CACHE["nc"] = build_program()
    nc = _NC_CACHE["nc"]
    in_maps = []
    for c in range(8):
        in_maps.append({"x": x[c], "wqkv": wqkv, "wo": wo, "gw": gw, "uw": uw,
                        "dw": dw, "cst": cst, "ones1": ones1, "identf": idf})
    res = run_bass_kernel_spmd(nc, in_maps, core_ids=list(range(8)))
    return np.stack([res.results[c]["out"] for c in range(8)], axis=0)


# version 06
# speedup vs baseline: 1.0099x; 1.0099x over previous
import contextlib
import numpy as np
import concourse.bass as bass
import concourse.mybir as mybir
from concourse.bass_utils import run_bass_kernel_spmd

F32 = mybir.dt.float32
BF16 = mybir.dt.bfloat16
I32 = mybir.dt.int32
Alu = mybir.AluOpType
Act = mybir.ActivationFunctionType
AX = mybir.AxisListType

S = 2048
H = 2048
NH = 16
FI = 8192
NTT = S // 128
QL = 512

_D = np.sqrt(np.float32(128.0)).astype(np.float32)
_INVD = 1.0 / np.float64(_D)
CW_HI = float(np.float32(_INVD))
CW_LO = float(np.float32(np.float64(_INVD) - np.float64(np.float32(_INVD))))
EPS_LN = float(np.float32(1e-5))
INV_H = 1.0 / 2048.0

C_448, C_NEG8 = 0, 1
CST_VALS = [448.0, -8.0]


class Sem:
    def __init__(self, name):
        self.name = name
        self.h = None
        self.n = 0

    def inc(self, k):
        self.n += k
        return self.n


class Prog:
    ENGINES = ("sync", "vector", "scalar", "tensor", "gpsimd")

    def __init__(self):
        self.q = {e: [] for e in self.ENGINES}

    def on(self, eng, fn, waits=(), inc=None, inc_n=1):
        w = [(s, v) for (s, v) in waits if s is not None and v > 0]
        self.q[eng].append((fn, w, inc, inc_n))
        return inc.inc(inc_n) if inc is not None else None


DEBUG_SPILLS = ()
SAFE_V = True
SAFE_S = False
SAFE_DMA = False


def build_program():
    nc = bass.Bass("TRN2", target_bir_lowering=False, debug=False,
                   detect_race_conditions=False)
    P = Prog()

    def _internal(name, shape, dtype):
        kind = "ExternalOutput" if name in DEBUG_SPILLS else "Internal"
        return nc.dram_tensor(name, shape, dtype, kind=kind).ap()

    d_x = nc.dram_tensor("x", [S, H], F32, kind="ExternalInput").ap()
    d_wqkv = nc.dram_tensor("wqkv", [H, 3 * H], F32, kind="ExternalInput").ap()
    d_wo = nc.dram_tensor("wo", [H, H], F32, kind="ExternalInput").ap()
    d_gw = nc.dram_tensor("gw", [FI, H], F32, kind="ExternalInput").ap()
    d_uw = nc.dram_tensor("uw", [FI, H], F32, kind="ExternalInput").ap()
    d_dw = nc.dram_tensor("dw", [H, FI], F32, kind="ExternalInput").ap()
    d_cst = nc.dram_tensor("cst", [128, len(CST_VALS)], F32, kind="ExternalInput").ap()
    d_ones = nc.dram_tensor("ones1", [1, 128], F32, kind="ExternalInput").ap()
    d_idf = nc.dram_tensor("identf", [128, 128], F32, kind="ExternalInput").ap()
    d_out = nc.dram_tensor("out", [S, H], F32, kind="ExternalOutput").ap()

    d_wqkvB = _internal("wqkvB", [48, 16, 128, 128], F32)
    d_swo_hi = _internal("swo_hi", [16, 4, 128, 512], BF16)
    d_swo_lo = _internal("swo_lo", [16, 4, 128, 512], BF16)
    d_sgw_hi = _internal("sgw_hi", [64, 4, 128, 512], BF16)
    d_sgw_lo = _internal("sgw_lo", [64, 4, 128, 512], BF16)
    d_suw_hi = _internal("suw_hi", [64, 4, 128, 512], BF16)
    d_suw_lo = _internal("suw_lo", [64, 4, 128, 512], BF16)
    d_sdw_hi = _internal("sdw_hi", [16, 16, 128, 512], BF16)
    d_sdw_lo = _internal("sdw_lo", [16, 16, 128, 512], BF16)
    d_gwT_hi = _internal("gwT_hi", [64, 128, 2048], BF16)
    d_gwT_lo = _internal("gwT_lo", [64, 128, 2048], BF16)
    d_uwT_hi = _internal("uwT_hi", [64, 128, 2048], BF16)
    d_uwT_lo = _internal("uwT_lo", [64, 128, 2048], BF16)
    d_dwT_hi = _internal("dwT_hi", [64, 16, 128, 128], BF16)
    d_dwT_lo = _internal("dwT_lo", [64, 16, 128, 128], BF16)
    d_xqT_hi = _internal("xqT_hi", [NTT, 4, 128, 512], BF16)
    d_xqT_lo = _internal("xqT_lo", [NTT, 4, 128, 512], BF16)
    d_hqT_hi = _internal("hqT_hi", [NTT, 16, 128, 512], BF16)
    d_hqT_lo = _internal("hqT_lo", [NTT, 16, 128, 512], BF16)
    d_sval_hi = _internal("sval_hi", [NTT, NH, 128, 128], BF16)
    d_sval_lo = _internal("sval_lo", [NTT, NH, 128, 128], BF16)
    d_sattnB = _internal("sattnB", [NTT, 4, 128, 512], F32)

    sT, sV, sS, ph = (Sem(n) for n in ("sT", "sV", "sS", "ph"))
    all_sems = [sT, sV, sS, ph]
    _streams = {}

    def stream(name):
        if name not in _streams:
            s_ = Sem(name)
            _streams[name] = s_
            all_sems.append(s_)
        return _streams[name]

    def dma_in(sem, out_ap, in_ap, waits=(), q="sync"):
        return P.on(q, lambda e, o=out_ap, i=in_ap: e.dma_start(out=o, in_=i),
                    waits, sem, 16)

    def dma_wr(sem, out_ap, in_ap, waits=()):
        return P.on("gpsimd", lambda e, o=out_ap, i=in_ap:
                    e.dma_start(out=o, in_=i), waits, sem, 16)

    def onV(fn, waits=()):
        w = list(waits)
        if SAFE_V:
            w.append((sV, sV.n))
        return P.on("vector", fn, w, sV, 1)

    def onS(fn, waits=()):
        w = list(waits)
        if SAFE_S:
            w.append((sS, sS.n))
        return P.on("scalar", fn, w, sS, 1)

    def onT(fn, waits=()):
        return P.on("tensor", fn, waits, sT, 1)

    def barrier():
        dw_ = [(s_, s_.n) for s_ in _streams.values()]
        for e in Prog.ENGINES:
            P.on(e, lambda en: en.nop(), dw_, ph, 1)
        tgt = ph.n
        for e in Prog.ENGINES:
            P.on(e, None, [(ph, tgt)])

    outer = contextlib.ExitStack()
    with outer:
        def sb(name, shape, dtype=F32, st=None):
            return (st or outer).enter_context(nc.sbuf_tensor(name, shape, dtype))

        cst_t = sb("cst_t", [128, len(CST_VALS)])
        r448 = sb("r448", [128, 1])
        ones_t = sb("ones_t", [1, 128])
        identF = sb("identF", [128, 128], F32)
        ident = sb("ident", [128, 128], BF16)
        q_sp = sb("q_sp", [128, 1])
        q_ya = sb("q_ya", [128, QL])
        q_v = sb("q_v", [128, QL])
        q_eb = sb("q_eb", [128, QL], I32)
        q_ib = sb("q_ib", [128, QL], I32)
        q_qb = sb("q_qb", [128, QL], I32)
        q_hi = sb("q_hi", [128, QL], BF16)
        q_lo = sb("q_lo", [128, QL], BF16)
        q_m = sb("q_m", [128, QL // 32])
        q_s = sb("q_s", [128, QL // 32])
        q_rs = sb("q_rs", [128, QL // 32])

        rCST = stream("rCST")
        dma_in(rCST, cst_t[:], d_cst)
        dma_in(rCST, ones_t[:], d_ones)
        id_rdy = dma_in(rCST, identF[:], d_idf)
        sI = onS(lambda e: e.copy(out=ident[:], in_=identF[:]), [(rCST, id_rdy)])
        onV(lambda e: e.tensor_scalar(
            out=q_sp[:], in0=cst_t[:, C_448:C_448 + 1], scalar1=1.0, scalar2=None,
            op0=Alu.mult), [(rCST, id_rdy)])
        r448_rdy = onV(lambda e: e.reciprocal(out=r448[:],
                                              in_=cst_t[:, C_448:C_448 + 1]))

        P_A = outer.enter_context(nc.psum_tensor("P_A", [128, 2048], F32))

        rPREP = stream("rPREP")
        wQ = stream("wQ")
        qws = {"w": []}

        def emit_quant(src_ap, ws, src_waits=()):
            L = QL
            nb = L // 32
            w_src = list(src_waits)
            qt_, qu_, qf_ = ws["t"], ws["u"], ws["f"]
            sya = onS(lambda e, a=src_ap: e.activation(
                out=q_ya[:], in_=a, func=Act.Abs), w_src + [(sV, sV.n)])
            onV(lambda e, a=src_ap: e.tensor_reduce(
                out=q_m[:], in_=a.rearrange("p (b k) -> p b k", k=32),
                axis=AX.X, op=Alu.max, apply_absolute_value=True), w_src)
            onV(lambda e: e.tensor_scalar(
                out=q_s[:], in0=q_m[:], scalar1=r448[:], scalar2=1e-30,
                op0=Alu.mult, op1=Alu.max), [(sV, r448_rdy)])
            onV(lambda e: e.tensor_scalar(out=q_sp[:], in0=q_s[:, 0:1],
                                          scalar1=1.0, scalar2=None,
                                          op0=Alu.mult))
            vrs = onV(lambda e: e.reciprocal(out=q_rs[:], in_=q_s[:]))
            vt = onV(lambda e, qt_=qt_: e.tensor_tensor(
                out=qt_.rearrange("p (b k) -> p b k", k=32),
                in0=q_ya[:].rearrange("p (b k) -> p b k", k=32),
                in1=q_rs[:].unsqueeze(2).broadcast_to([128, nb, 32]),
                op=Alu.mult), [(sS, sya), (sV, vrs)])
            onV(lambda e, qt_=qt_: e.tensor_scalar(
                out=q_eb[:], in0=qt_.bitcast(I32), scalar1=23, scalar2=None,
                op0=Alu.logical_shift_right))
            onV(lambda e: e.tensor_scalar(
                out=q_eb[:], in0=q_eb[:], scalar1=121.0, scalar2=None,
                op0=Alu.max))
            onV(lambda e: e.tensor_scalar(
                out=q_ib[:], in0=q_eb[:], scalar1=-1.0, scalar2=257.0,
                op0=Alu.mult, op1=Alu.add))
            onV(lambda e: e.tensor_scalar(
                out=q_ib[:], in0=q_ib[:], scalar1=23, scalar2=None,
                op0=Alu.logical_shift_left))
            onV(lambda e: e.tensor_scalar(
                out=q_qb[:], in0=q_eb[:], scalar1=3.0, scalar2=None,
                op0=Alu.subtract))
            onV(lambda e: e.tensor_scalar(
                out=q_qb[:], in0=q_qb[:], scalar1=23, scalar2=None,
                op0=Alu.logical_shift_left))
            onV(lambda e, qt_=qt_, qu_=qu_: e.tensor_tensor(
                out=qu_, in0=qt_, in1=q_ib[:].bitcast(F32), op=Alu.mult))
            onV(lambda e, qu_=qu_: e.tensor_copy(out=q_eb[:], in_=qu_))
            onV(lambda e, qu_=qu_, qf_=qf_: e.tensor_tensor(
                out=qf_, in0=qu_, in1=q_eb[:], op=Alu.subtract))
            onV(lambda e, qf_=qf_: e.scalar_tensor_tensor(
                out=q_v[:], in0=qf_, scalar=0.5, in1=q_eb[:],
                op0=Alu.is_ge, op1=Alu.add))
            onV(lambda e, qu_=qu_: e.tensor_tensor(
                out=qu_, in0=q_v[:], in1=q_qb[:].bitcast(F32), op=Alu.mult))
            onV(lambda e, qt_=qt_, qu_=qu_: e.tensor_tensor(
                out=qt_.rearrange("p (b k) -> p b k", k=32),
                in0=qu_.rearrange("p (b k) -> p b k", k=32),
                in1=q_s[:].unsqueeze(2).broadcast_to([128, nb, 32]),
                op=Alu.mult))
            ssg = onS(lambda e, a=src_ap: e.activation(
                out=q_ya[:], in_=a, func=Act.Sign), [(sV, vt)])
            vdq = onV(lambda e, qt_=qt_, qf_=qf_: e.tensor_tensor(
                out=qf_, in0=qt_, in1=q_ya[:], op=Alu.mult), [(sS, ssg)])
            shi = onS(lambda e, qf_=qf_: e.copy(out=q_hi[:], in_=qf_),
                      [(sV, vdq)] + list(qws["w"]))
            vlo = onV(lambda e, qf_=qf_: e.tensor_tensor(
                out=q_lo[:], in0=qf_, in1=q_hi[:], op=Alu.subtract),
                [(sS, shi)])
            return shi, vlo

        prep_jobs = ([("wo", i) for i in range(16)]
                     + [("gw", i) for i in range(64)]
                     + [("uw", i) for i in range(64)]
                     + [("dw", i) for i in range(64)])
        prep_state = {"i": 0, "s_done": 0, "wr": {}}
        prep_env = {}

        def emit_prep_steps(k):
            for _ in range(k):
                i = prep_state["i"]
                if i >= len(prep_jobs):
                    return
                prep_state["i"] = i + 1
                tag, idx = prep_jobs[i]
                strip = prep_env["strip"]
                if tag == "wo":
                    src = d_wo[idx * 128:(idx + 1) * 128, :]
                    blks = [(d_swo_hi[idx, b], d_swo_lo[idx, b]) for b in range(4)]
                elif tag == "gw":
                    src = d_gw[idx * 128:(idx + 1) * 128, :]
                    blks = [(d_sgw_hi[idx, b], d_sgw_lo[idx, b]) for b in range(4)]
                elif tag == "uw":
                    src = d_uw[idx * 128:(idx + 1) * 128, :]
                    blks = [(d_suw_hi[idx, b], d_suw_lo[idx, b]) for b in range(4)]
                else:
                    s_i, c4 = idx // 4, idx % 4
                    src = d_dw[s_i * 128:(s_i + 1) * 128,
                               c4 * 2048:(c4 + 1) * 2048]
                    blks = [(d_sdw_hi[s_i, c4 * 4 + b], d_sdw_lo[s_i, c4 * 4 + b])
                            for b in range(4)]
                lw = list(qws["w"]) + prep_env["ld_extra"]()
                ld = dma_in(rPREP, strip[:], src, lw, q="gpsimd")
                shi = 0
                for b in range(4):
                    sap = strip[:, b * 512:(b + 1) * 512]
                    if tag == "wo":
                        shi = onS(lambda e, a=sap: e.copy(out=q_hi[:], in_=a),
                                  [(rPREP, ld), (sV, sV.n)] + list(qws["w"]))
                        vlo = onV(lambda e, a=sap: e.tensor_tensor(
                            out=q_lo[:], in0=a, in1=q_hi[:], op=Alu.subtract),
                            [(sS, shi)])
                    else:
                        shi, vlo = emit_quant(sap, prep_env["ws"], [(rPREP, ld)])
                    dma_wr(wQ, blks[b][0][:, :], q_hi[:], [(sS, shi)])
                    w2 = dma_wr(wQ, blks[b][1][:, :], q_lo[:], [(sV, vlo)])
                    qws["w"] = [(wQ, w2)]
                prep_state["s_done"] = shi
                prep_state["wr"][(tag, idx)] = w2

        rWB = stream("rWB")
        wWB = stream("wWB")
        rXT = stream("rXT")
        wXQT = stream("wXQT")

        with contextlib.ExitStack() as ph_aa:
            xn_fm_hi = sb("xn_fm_hi", [128, 16 * 2048], BF16, ph_aa)
            xn_fm_lo = sb("xn_fm_lo", [128, 16 * 2048], BF16, ph_aa)
            fm_hi_r = xn_fm_hi[:].rearrange("p (c t) -> p c t", t=2048)
            fm_lo_r = xn_fm_lo[:].rearrange("p (c t) -> p c t", t=2048)

            with contextlib.ExitStack() as ph_a0:
                xt = sb("xt", [128, 2048], F32, ph_a0)
                xn = sb("xn", [128, 2048], F32, ph_a0)
                xnhi = sb("xnhi", [128, 2048], BF16, ph_a0)
                xnlo = sb("xnlo", [128, 2048], BF16, ph_a0)
                xqs_hi = sb("xqs_hi", [128, 512], BF16, ph_a0)
                xqs_lo = sb("xqs_lo", [128, 512], BF16, ph_a0)
                wb_stg = sb("wb_stg", [128, 3072], F32, ph_a0)
                a0_t = sb("a0_t", [128, QL], F32, ph_a0)
                a0_u = sb("a0_u", [128, QL], F32, ph_a0)
                a0_f = sb("a0_f", [128, QL], F32, ph_a0)
                a0_strip = sb("a0_strip", [128, 2048], F32, ph_a0)
                lnsum = sb("lnsum", [128, 1], F32, ph_a0)
                lnm = sb("lnm", [128, 1], F32, ph_a0)
                lnnm = sb("lnnm", [128, 1], F32, ph_a0)
                lnvs = sb("lnvs", [128, 1], F32, ph_a0)
                lnsd = sb("lnsd", [128, 1], F32, ph_a0)
                lnr = sb("lnr", [128, 1], F32, ph_a0)
                lnb = sb("lnb", [128, 1], F32, ph_a0)
                tpb = [ph_a0.enter_context(
                    nc.psum_tensor(f"TPB{i}", [128, 1024], BF16))
                    for i in range(4)]

                prep_env["strip"] = a0_strip
                prep_env["ws"] = {"t": a0_t[:], "u": a0_u[:], "f": a0_f[:]}
                prep_env["ld_extra"] = lambda: []

                tp_free = [0, 0, 0, 0]
                tp_ctr = [0]
                xqs_wr = [0]
                xt_free = {"v": 0, "s": 0}
                xn_free_v = 0
                xnsplit_t = 0

                def tp_batch4(src_tiles, evac_fn, gate, ev_extra=()):
                    bk = tp_ctr[0] % 4
                    tp_ctr[0] += 1
                    psb = tpb[bk][:, 0:512]
                    w0 = list(gate) + [(sS, sI), (sV, tp_free[bk])]
                    tmm = 0
                    for k4, tile in enumerate(src_tiles):
                        tmm = onT(lambda e, t_=tile, k4=k4, psb=psb:
                                  e.transpose(
                            out=psb[:, k4 * 128:(k4 + 1) * 128],
                            in_=t_, identity=ident[:]), w0)
                        w0 = ()
                    ev = onV(lambda e, psb=psb, f=evac_fn: f(
                        e, psb[:, 0:512]), [(sT, tmm)] + list(ev_extra))
                    tp_free[bk] = ev
                    return tmm, ev

                for tt in range(NTT):
                    ld = dma_in(rXT, xt[:], d_x[tt * 128:(tt + 1) * 128, :],
                                [(sV, xt_free["v"]), (sS, xt_free["s"])])
                    for half in range(2):
                        lwb = dma_in(rWB, wb_stg[:],
                                     d_wqkv[tt * 128:(tt + 1) * 128,
                                            half * 3072:(half + 1) * 3072],
                                     [(wWB, wWB.n)], q="gpsimd")
                        for k in range(24):
                            fi = half * 24 + k
                            dma_wr(wWB, d_wqkvB[fi, tt, :, :],
                                   wb_stg[:, k * 128:(k + 1) * 128],
                                   [(rWB, lwb)] if k == 0 else ())
                    onV(lambda e: e.reduce_sum(out=lnsum[:], in_=xt[:],
                                               axis=AX.X), [(rXT, ld)])
                    onV(lambda e: e.tensor_scalar(out=lnm[:], in0=lnsum[:],
                                                  scalar1=INV_H, scalar2=None,
                                                  op0=Alu.mult))
                    vnm = onV(lambda e: e.tensor_scalar(
                        out=lnnm[:], in0=lnm[:], scalar1=-1.0, scalar2=None,
                        op0=Alu.mult))
                    ssq = onS(lambda e: e.activation(
                        out=P_A[:, 0:2048], in_=xt[:], func=Act.Square,
                        bias=lnnm[:], accum_out=lnvs[:]),
                        [(sV, vnm), (rXT, ld)])
                    vv1 = onV(lambda e: e.tensor_scalar(
                        out=lnvs[:], in0=lnvs[:], scalar1=INV_H, scalar2=EPS_LN,
                        op0=Alu.mult, op1=Alu.add), [(sS, ssq)])
                    ssd = onS(lambda e: e.activation(out=lnsd[:], in_=lnvs[:],
                                                     func=Act.Sqrt), [(sV, vv1)])
                    onV(lambda e: e.tensor_scalar(out=q_sp[:], in0=lnsd[:],
                                                  scalar1=1.0, scalar2=None,
                                                  op0=Alu.mult), [(sS, ssd)])
                    vr = onV(lambda e: e.reciprocal(out=lnr[:], in_=lnsd[:]))
                    vb = onV(lambda e: e.tensor_tensor(
                        out=lnb[:], in0=lnnm[:], in1=lnr[:], op=Alu.mult),
                        [(sV, vr)])
                    sxn = onS(lambda e: e.activation(
                        out=xn[:], in_=xt[:], func=Act.Identity, scale=lnr[:],
                        bias=lnb[:]), [(sV, vb), (sV, xn_free_v)])
                    sxh = onS(lambda e: e.copy(out=xnhi[:], in_=xn[:]),
                              [(sT, xnsplit_t)])
                    vxl = onV(lambda e: e.tensor_tensor(
                        out=xnlo[:], in0=xn[:], in1=xnhi[:], op=Alu.subtract),
                        [(sS, sxh), (sT, xnsplit_t)])
                    t_last = 0
                    for b4 in range(4):
                        hcs = [b4 * 4 + k for k in range(4)]
                        t_last, _ = tp_batch4(
                            [xnhi[:, hc * 128:(hc + 1) * 128] for hc in hcs],
                            lambda e, ps, b4=b4, tt=tt: e.tensor_copy(
                                out=fm_hi_r[:, b4 * 4:(b4 + 1) * 4,
                                            tt * 128:(tt + 1) * 128],
                                in_=ps.rearrange("p (c t) -> p c t", t=128)),
                            [(sV, vxl)])
                        t_last, _ = tp_batch4(
                            [xnlo[:, hc * 128:(hc + 1) * 128] for hc in hcs],
                            lambda e, ps, b4=b4, tt=tt: e.tensor_copy(
                                out=fm_lo_r[:, b4 * 4:(b4 + 1) * 4,
                                            tt * 128:(tt + 1) * 128],
                                in_=ps.rearrange("p (c t) -> p c t", t=128)),
                            [(sV, vxl)])
                    xnsplit_t = t_last
                    last_shi, last_vlo = 0, 0
                    for hf in range(4):
                        shi, vlo = emit_quant(xn[:, hf * 512:(hf + 1) * 512],
                                              prep_env["ws"], [(sS, sxn)])
                        qtm, evh = tp_batch4(
                            [q_hi[:, k * 128:(k + 1) * 128] for k in range(4)],
                            lambda e, ps: e.tensor_copy(out=xqs_hi[:], in_=ps),
                            [(sS, shi)], ev_extra=[(wXQT, xqs_wr[0])])
                        qtm, evl = tp_batch4(
                            [q_lo[:, k * 128:(k + 1) * 128] for k in range(4)],
                            lambda e, ps: e.tensor_copy(out=xqs_lo[:], in_=ps),
                            [(sV, vlo)], ev_extra=[(wXQT, xqs_wr[0])])
                        qws["w"] = [(sT, qtm)]
                        dma_wr(wXQT, d_xqT_hi[tt, hf, :, :], xqs_hi[:],
                               [(sV, evh)])
                        xqs_wr[0] = dma_wr(wXQT, d_xqT_lo[tt, hf, :, :],
                                           xqs_lo[:], [(sV, evl)])
                        last_shi, last_vlo = shi, vlo
                    emit_prep_steps(1)
                    xt_free = {"v": vxl, "s": last_shi}
                    xn_free_v = last_vlo

            barrier()

            with contextlib.ExitStack() as ph_a:
                wsl = sb("wsl", [128, 2048], F32, ph_a)
                whi = sb("whi", [128, 2048], BF16, ph_a)
                wlo = sb("wlo", [128, 2048], BF16, ph_a)
                Qhi = sb("Qhi", [128, 2048], BF16, ph_a)
                Qlo = sb("Qlo", [128, 2048], BF16, ph_a)
                Khi = sb("Khi", [128, 2048], BF16, ph_a)
                Klo = sb("Klo", [128, 2048], BF16, ph_a)
                Vfm = sb("Vfm", [128, 2048], F32, ph_a)
                E_t = [sb(f"E{i}", [128, 2048], F32, ph_a) for i in range(2)]
                valhi = sb("valhi", [128, 512], BF16, ph_a)
                vallo = sb("vallo", [128, 512], BF16, ph_a)
                rw4 = sb("rw4", [128, 8], F32, ph_a)
                rws = sb("rws", [128, 2], F32, ph_a)
                r_t = sb("r_t", [128, 2], F32, ph_a)
                c_ps = ph_a.enter_context(
                    nc.psum_tensor("PCL", [128, 2048], F32))

                c_row = E_t[0][0:1, :]
                prep_env["strip"] = E_t[0]
                prep_env["ws"] = {"t": Vfm[:, 0:512], "u": Vfm[:, 512:1024],
                                  "f": Vfm[:, 1024:1536]}
                prep_env["ld_extra"] = lambda: [(sT, sT.n)]

                rWSL = stream("rWSL")
                wVAL = stream("wVAL")
                wsl_free = {"v": 0, "s": 0}
                w_mm_free = 0
                qkv_ps = [(0, 0), (0, 0)]
                sc_ps = [(0, 0), (0, 0)]
                E_rd = [0, 0]
                rw_free = [0, 0]
                rt_busy = [0, 0]
                rt_ready = [0, 0]
                val_state = {"s": 0, "wr": 0}

                for hd in range(NH):
                    qk_gate = (0, 0)
                    for comp in range(3):
                        fi = comp * 16 + hd
                        lw0 = [(sS, wsl_free["s"]), (sV, wsl_free["v"])]
                        for hc in range(16):
                            dma_in(rWSL, wsl[:, hc * 128:(hc + 1) * 128],
                                   d_wqkvB[fi, hc, :, :], lw0)
                            lw0 = ()
                        wld = rWSL.n
                        if comp == 0:
                            vsc = 0
                            for p4 in range(4):
                                onV(lambda e, p4=p4: e.tensor_scalar(
                                    out=q_v[:],
                                    in0=wsl[:, p4 * 512:(p4 + 1) * 512],
                                    scalar1=CW_LO, scalar2=None, op0=Alu.mult),
                                    [(rWSL, wld)] if p4 == 0 else ())
                                vsc = onV(lambda e, p4=p4: e.scalar_tensor_tensor(
                                    out=wsl[:, p4 * 512:(p4 + 1) * 512],
                                    in0=wsl[:, p4 * 512:(p4 + 1) * 512],
                                    scalar=CW_HI, in1=q_v[:],
                                    op0=Alu.mult, op1=Alu.add))
                            wait_w = [(sV, vsc)]
                        else:
                            wait_w = [(rWSL, wld)]
                        swh = onS(lambda e: e.copy(out=whi[:], in_=wsl[:]),
                                  wait_w + [(sT, w_mm_free)])
                        vwl = onV(lambda e: e.tensor_tensor(
                            out=wlo[:], in0=wsl[:], in1=whi[:], op=Alu.subtract),
                            [(sS, swh), (sT, w_mm_free)])
                        wsl_free = {"v": vwl, "s": swh}
                        for tc in range(4):
                            qb = (comp * 4 + tc) % 2
                            wq = [(sV, vwl), (sV, qkv_ps[qb][0]),
                                  (sS, qkv_ps[qb][1])]
                            mm = 0
                            for hc in range(16):
                                for pi, (wt, xf) in enumerate(
                                        ((whi, fm_hi_r), (whi, fm_lo_r),
                                         (wlo, fm_hi_r))):
                                    mm = onT(lambda e, wt=wt, xf=xf, hc=hc,
                                             tc=tc, qb=qb,
                                             st=(hc == 0 and pi == 0),
                                             sp=(hc == 15 and pi == 2):
                                             e.matmul(
                                        out=P_A[:, qb * 512:(qb + 1) * 512],
                                        lhsT=wt[:, hc * 128:(hc + 1) * 128],
                                        rhs=xf[:, hc, tc * 512:(tc + 1) * 512],
                                        start=st, stop=sp), wq)
                                    wq = ()
                            w_mm_free = mm
                            if comp < 2:
                                dsth, dstl = ((Qhi, Qlo) if comp == 0
                                              else (Khi, Klo))
                                sh = onS(lambda e, d=dsth, tc=tc, qb=qb: e.copy(
                                    out=d[:, tc * 512:(tc + 1) * 512],
                                    in_=P_A[:, qb * 512:(qb + 1) * 512]),
                                    [(sT, mm)])
                                vl = onV(lambda e, dh=dsth, dl=dstl, tc=tc,
                                         qb=qb: e.tensor_tensor(
                                             out=dl[:, tc * 512:(tc + 1) * 512],
                                             in0=P_A[:, qb * 512:(qb + 1) * 512],
                                             in1=dh[:, tc * 512:(tc + 1) * 512],
                                             op=Alu.subtract), [(sS, sh)])
                                qkv_ps[qb] = (vl, sh)
                                if comp == 1:
                                    qk_gate = (vl, sh)
                            else:
                                vl = onV(lambda e, tc=tc, qb=qb: e.tensor_copy(
                                    out=Vfm[:, tc * 512:(tc + 1) * 512],
                                    in_=P_A[:, qb * 512:(qb + 1) * 512]),
                                    [(sT, mm), (sS, prep_state["s_done"])])
                                qkv_ps[qb] = (vl, qkv_ps[qb][1])
                        if comp < 2:
                            emit_prep_steps(2)
                    col_mm = [0]

                    def emit_colsum(qtc):
                        ebc = qtc % 2
                        w0 = [(sV, rt_ready[ebc])]
                        mmc = 0
                        for kc in range(4):
                            mmc = onT(lambda e, ebc=ebc, kc=kc, qtc=qtc:
                                      e.matmul(
                                out=c_ps[0:1, kc * 512:(kc + 1) * 512],
                                lhsT=r_t[:, ebc:ebc + 1],
                                rhs=E_t[ebc][:, kc * 512:(kc + 1) * 512],
                                start=(qtc == 0), stop=(qtc == 15)), w0)
                            w0 = ()
                        rt_busy[ebc] = mmc
                        E_rd[ebc] = mmc
                        col_mm[0] = mmc

                    se = 0
                    for qt in range(16):
                        eb = qt % 2
                        for kc in range(4):
                            b2 = kc % 2
                            w0 = [(sS, sc_ps[b2][1]), (sV, sc_ps[b2][0])]
                            if qt == 0 and kc == 0:
                                w0 += [(sV, qk_gate[0]), (sS, qk_gate[1])]
                            mm = 0
                            for pi, (qq, kk) in enumerate(
                                    ((Qhi, Khi), (Qhi, Klo), (Qlo, Khi))):
                                mm = onT(lambda e, qq=qq, kk=kk, qt=qt, kc=kc,
                                         b2=b2, st=(pi == 0), sp=(pi == 2):
                                         e.matmul(
                                    out=P_A[:, (2 + b2) * 512:(3 + b2) * 512],
                                    lhsT=qq[:, qt * 128:(qt + 1) * 128],
                                    rhs=kk[:, kc * 512:(kc + 1) * 512],
                                    start=st, stop=sp), w0)
                                w0 = ()
                            we = [(sT, mm)]
                            if kc == 0:
                                we += [(sV, rw_free[eb]), (sT, E_rd[eb])]
                                if qt == 0:
                                    we.append((rPREP, rPREP.n))
                            se = onS(lambda e, eb=eb, kc=kc, b2=b2:
                                     e.activation(
                                out=E_t[eb][:, kc * 512:(kc + 1) * 512],
                                in_=P_A[:, (2 + b2) * 512:(3 + b2) * 512],
                                func=Act.Exp, bias=cst_t[:, C_NEG8:C_NEG8 + 1],
                                accum_out=rw4[:, eb * 4 + kc:eb * 4 + kc + 1]),
                                we)
                            sc_ps[b2] = (sc_ps[b2][0], se)
                        vr1 = onV(lambda e, eb=eb: e.reduce_sum(
                            out=rws[:, eb:eb + 1],
                            in_=rw4[:, eb * 4:eb * 4 + 4], axis=AX.X),
                            [(sS, se)])
                        rw_free[eb] = vr1
                        onV(lambda e, eb=eb: e.tensor_scalar(
                            out=q_sp[:], in0=rws[:, eb:eb + 1], scalar1=1.0,
                            scalar2=None, op0=Alu.mult))
                        vrr = onV(lambda e, eb=eb: e.reciprocal(
                            out=r_t[:, eb:eb + 1], in_=rws[:, eb:eb + 1]),
                            [(sT, rt_busy[eb])])
                        rt_ready[eb] = vrr
                        if qt > 0:
                            emit_colsum(qt - 1)
                    emit_colsum(15)
                    vc = onV(lambda e: e.tensor_copy(out=c_row,
                                                     in_=c_ps[0:1, :]),
                             [(sT, col_mm[0])])
                    crep_last = 0
                    for kc in range(4):
                        b2 = kc % 2
                        crep_mm = onT(lambda e, kc=kc, b2=b2: e.matmul(
                            out=P_A[:, (2 + b2) * 512:(3 + b2) * 512],
                            lhsT=ones_t[:],
                            rhs=c_row[0:1, kc * 512:(kc + 1) * 512],
                            start=True, stop=True),
                            [(sV, vc), (sS, sc_ps[b2][1]), (sV, sc_ps[b2][0])])
                        crep_last = crep_mm
                        vv = onV(lambda e, kc=kc, b2=b2: e.tensor_tensor(
                            out=q_ya[:], in0=Vfm[:, kc * 512:(kc + 1) * 512],
                            in1=P_A[:, (2 + b2) * 512:(3 + b2) * 512],
                            op=Alu.mult),
                            [(sT, crep_mm), (sS, prep_state["s_done"]),
                             (sS, val_state["s"])])
                        svh = onS(lambda e: e.copy(out=valhi[:], in_=q_ya[:]),
                                  [(sV, vv), (wVAL, val_state["wr"])])
                        vvl = onV(lambda e: e.tensor_tensor(
                            out=vallo[:], in0=q_ya[:], in1=valhi[:],
                            op=Alu.subtract), [(sS, svh)])
                        wv = 0
                        for tq in range(4):
                            ti = kc * 4 + tq
                            dma_wr(wVAL, d_sval_hi[ti, hd, :, :],
                                   valhi[:, tq * 128:(tq + 1) * 128],
                                   [(sS, svh)] if tq == 0 else ())
                            wv = dma_wr(wVAL, d_sval_lo[ti, hd, :, :],
                                        vallo[:, tq * 128:(tq + 1) * 128],
                                        [(sV, vvl)] if tq == 0 else ())
                        val_state = {"s": svh, "wr": wv}
                        sc_ps[b2] = (vv, sc_ps[b2][1])
                    E_rd[0] = crep_last
                    emit_prep_steps(4)
                if prep_state["i"] < 144:
                    emit_prep_steps(144 - prep_state["i"])

        barrier()

        with contextlib.ExitStack() as ph_b:
            wo_hi = sb("wo_hi", [128, 16 * 1024], BF16, ph_b)
            wo_lo = sb("wo_lo", [128, 16 * 1024], BF16, ph_b)
            vt_hi = [sb(f"vt_hi{i}", [128, 2048], BF16, ph_b) for i in range(2)]
            vt_lo = [sb(f"vt_lo{i}", [128, 2048], BF16, ph_b) for i in range(2)]
            ao = sb("ao", [128, 1024], F32, ph_b)
            ml_in = [sb(f"ml_in{i}", [128, 2048], BF16, ph_b) for i in range(2)]
            ml_st = [sb(f"ml_st{i}", [128, 2048], BF16, ph_b) for i in range(2)]
            mlb = [ph_b.enter_context(
                nc.psum_tensor(f"MLB{i}", [128, 1024], BF16))
                for i in range(2)]

            rWO = stream("rWO")
            rVT = [stream("rVT0"), stream("rVT1")]
            wATTN = stream("wATTN")
            rMILL = [stream("rMILL0"), stream("rMILL1")]
            wMILL = [stream("wMILL0"), stream("wMILL1")]

            mill_jobs = [(srcb, dstb, s)
                         for (srcb, dstb) in ((d_sgw_hi, d_gwT_hi),
                                              (d_sgw_lo, d_gwT_lo),
                                              (d_suw_hi, d_uwT_hi),
                                              (d_suw_lo, d_uwT_lo))
                         for s in range(64)]
            mill_state = {"i": 0, "in_free": [0, 0], "st_free": [0, 0],
                          "ps_free": [0, 0]}

            def emit_mill_steps(k):
                for _ in range(k):
                    j = mill_state["i"]
                    if j >= len(mill_jobs):
                        return
                    mill_state["i"] = j + 1
                    srcb, dstb, s = mill_jobs[j]
                    sl = j % 2
                    ld = 0
                    lw = [(sT, mill_state["in_free"][sl])]
                    for c in range(4):
                        ld = dma_in(rMILL[sl],
                                    ml_in[sl][:, c * 512:(c + 1) * 512],
                                    srcb[s, c, :, :], lw)
                        lw = ()
                    tmm = 0
                    for b4 in range(4):
                        pb = b4 % 2
                        psb = mlb[pb][:, 0:512]
                        w0 = [(rMILL[sl], ld), (sS, sI),
                              (sV, mill_state["ps_free"][pb])]
                        for k4 in range(4):
                            hc = b4 * 4 + k4
                            tmm = onT(lambda e, hc=hc, k4=k4, psb=psb, sl=sl:
                                      e.transpose(
                                out=psb[:, k4 * 128:(k4 + 1) * 128],
                                in_=ml_in[sl][:, hc * 128:(hc + 1) * 128],
                                identity=ident[:]), w0)
                            w0 = ()
                        ev = onV(lambda e, b4=b4, psb=psb, sl=sl: e.tensor_copy(
                            out=ml_st[sl][:, b4 * 512:(b4 + 1) * 512],
                            in_=psb[:, 0:512]),
                            [(sT, tmm), (wMILL[sl], mill_state["st_free"][sl])])
                        mill_state["ps_free"][pb] = ev
                    mill_state["in_free"][sl] = tmm
                    wv = dma_wr(wMILL[sl], dstb[s, :, :], ml_st[sl][:],
                                [(sV, mill_state["ps_free"][0]),
                                 (sV, mill_state["ps_free"][1])])
                    mill_state["st_free"][sl] = wv

            wo_free = 0
            po_free = [0, 0]
            wr_attn = 0
            mm_l = 0
            for hh in range(2):
                lw = [(sT, wo_free)]
                for fc in range(16):
                    dma_in(rWO, wo_hi[:, fc * 1024:fc * 1024 + 512],
                           d_swo_hi[fc, hh * 2, :, :], lw)
                    lw = ()
                    dma_in(rWO, wo_hi[:, fc * 1024 + 512:(fc + 1) * 1024],
                           d_swo_hi[fc, hh * 2 + 1, :, :])
                    dma_in(rWO, wo_lo[:, fc * 1024:fc * 1024 + 512],
                           d_swo_lo[fc, hh * 2, :, :])
                    dma_in(rWO, wo_lo[:, fc * 1024 + 512:(fc + 1) * 1024],
                           d_swo_lo[fc, hh * 2 + 1, :, :])
                wo_ready = rWO.n
                vt_free = [0, 0]
                for tt in range(NTT):
                    vb = tt % 2
                    lw = [(sT, vt_free[vb])]
                    for f in range(16):
                        dma_in(rVT[vb], vt_hi[vb][:, f * 128:(f + 1) * 128],
                               d_sval_hi[tt, f, :, :], lw)
                        lw = ()
                        dma_in(rVT[vb], vt_lo[vb][:, f * 128:(f + 1) * 128],
                               d_sval_lo[tt, f, :, :])
                    vready = rVT[vb].n
                    for hcB in range(2):
                        pb = hcB
                        wq = [(rVT[vb], vready), (rWO, wo_ready),
                              (sV, po_free[pb])]
                        for fc in range(16):
                            for pi, (vv_, ww_) in enumerate(
                                    ((vt_hi[vb], wo_hi), (vt_hi[vb], wo_lo),
                                     (vt_lo[vb], wo_hi))):
                                mm_l = onT(lambda e, vv_=vv_, ww_=ww_, fc=fc,
                                           hcB=hcB, pb=pb,
                                           st=(fc == 0 and pi == 0),
                                           sp=(fc == 15 and pi == 2): e.matmul(
                                    out=P_A[:, pb * 512:(pb + 1) * 512],
                                    lhsT=vv_[:, fc * 128:(fc + 1) * 128],
                                    rhs=ww_[:, fc * 1024 + hcB * 512:
                                            fc * 1024 + (hcB + 1) * 512],
                                    start=st, stop=sp), wq)
                                wq = ()
                        pv = onV(lambda e, hcB=hcB, pb=pb: e.tensor_copy(
                            out=ao[:, hcB * 512:(hcB + 1) * 512],
                            in_=P_A[:, pb * 512:(pb + 1) * 512]),
                            [(sT, mm_l), (wATTN, wr_attn)])
                        po_free[pb] = pv
                    vt_free[vb] = mm_l
                    dma_wr(wATTN, d_sattnB[tt, hh * 2, :, :], ao[:, 0:512],
                           [(sV, po_free[0]), (sV, po_free[1])])
                    wr_attn = dma_wr(wATTN, d_sattnB[tt, hh * 2 + 1, :, :],
                                     ao[:, 512:1024], ())
                    emit_mill_steps(8)
                wo_free = mm_l
            emit_mill_steps(len(mill_jobs))

        barrier()

        with contextlib.ExitStack() as ph_c:
            xqg_hi = sb("xqg_hi", [128, 8192], BF16, ph_c)
            xqg_lo = sb("xqg_lo", [128, 8192], BF16, ph_c)
            ws_w = [[sb(f"wsC{i}_{j}", [128, 2048], BF16, ph_c)
                     for j in range(4)] for i in range(2)]
            hT = [sb(f"hT{i}", [128, 512], F32, ph_c) for i in range(2)]
            hstage = [[sb(f"hstg{g}_{i}", [128, 512], F32, ph_c)
                       for i in range(4)] for g in range(2)]
            hqs_hi = sb("hqs_hi", [128, 512], BF16, ph_c)
            hqs_lo = sb("hqs_lo", [128, 512], BF16, ph_c)
            gtmp = sb("gtmp", [128, 512], F32, ph_c)
            dml_in = [sb(f"dml_in{i}", [128, 2048], BF16, ph_c) for i in range(2)]
            dml_st = [sb(f"dml_st{i}", [128, 2048], BF16, ph_c) for i in range(2)]
            c_t = sb("c_t", [128, QL], F32, ph_c)
            c_u = sb("c_u", [128, QL], F32, ph_c)
            c_f = sb("c_f", [128, QL], F32, ph_c)
            c_strip = sb("c_strip", [128, 2048], F32, ph_c)
            tbf = [ph_c.enter_context(
                nc.psum_tensor(f"TBF{i}", [128, 1024], BF16)) for i in range(2)]
            htf = [ph_c.enter_context(
                nc.psum_tensor(f"HTF{i}", [128, 512], F32)) for i in range(2)]
            bf_ctr = [0]
            bf_free = [0, 0]

            prep_env["strip"] = c_strip
            prep_env["ws"] = {"t": c_t[:], "u": c_u[:], "f": c_f[:]}
            prep_env["ld_extra"] = lambda: []

            rXQG = stream("rXQG")
            rGWS = [stream("rGWS0"), stream("rGWS1")]
            rDML = [stream("rDML0"), stream("rDML1")]
            wDML = [stream("wDML0"), stream("wDML1")]
            wHQT = stream("wHQT")

            xqg_w_hi = xqg_hi[:].rearrange("p (a c4 q) -> p a c4 q", a=4, c4=4)
            xqg_w_lo = xqg_lo[:].rearrange("p (a c4 q) -> p a c4 q", a=4, c4=4)

            dml_jobs = [(srcb, dstb, idx)
                        for (srcb, dstb) in ((d_sdw_hi, d_dwT_hi),
                                             (d_sdw_lo, d_dwT_lo))
                        for idx in range(64)]
            dml_state = {"i": 0, "in_free": [0, 0], "st_free": [0, 0],
                         "ps_free": [0, 0]}

            def emit_dml_steps(k):
                for _ in range(k):
                    j = dml_state["i"]
                    if j >= len(dml_jobs):
                        return
                    dml_state["i"] = j + 1
                    srcb, dstb, idx = dml_jobs[j]
                    s_i, c4 = idx // 4, idx % 4
                    sl = j % 2
                    ld = 0
                    lw = [(sT, dml_state["in_free"][sl]),
                          (wQ, prep_state["wr"][("dw", idx)])]
                    for c in range(4):
                        ld = dma_in(rDML[sl],
                                    dml_in[sl][:, c * 512:(c + 1) * 512],
                                    srcb[s_i, c4 * 4 + c, :, :], lw)
                        lw = ()
                    tmm = 0
                    ev = 0
                    for b4 in range(4):
                        r = bf_ctr[0] % 2
                        bf_ctr[0] += 1
                        psb = tbf[r][:, 0:512]
                        w0 = [(rDML[sl], ld), (sS, sI), (sV, bf_free[r])]
                        for k4 in range(4):
                            icc = b4 * 4 + k4
                            tmm = onT(lambda e, icc=icc, k4=k4, psb=psb, sl=sl:
                                      e.transpose(
                                out=psb[:, k4 * 128:(k4 + 1) * 128],
                                in_=dml_in[sl][:, icc * 128:(icc + 1) * 128],
                                identity=ident[:]), w0)
                            w0 = ()
                        ev = onV(lambda e, b4=b4, psb=psb, sl=sl: e.tensor_copy(
                            out=dml_st[sl][:, b4 * 512:(b4 + 1) * 512],
                            in_=psb[:, 0:512]),
                            [(sT, tmm), (wDML[sl], dml_state["st_free"][sl])])
                        bf_free[r] = ev
                    dml_state["in_free"][sl] = tmm
                    wv = 0
                    evw = [(sV, ev)]
                    for icc in range(16):
                        wv = dma_wr(wDML[sl], dstb[c4 * 16 + icc, s_i, :, :],
                                    dml_st[sl][:, icc * 128:(icc + 1) * 128],
                                    evw)
                        evw = ()
                    dml_state["st_free"][sl] = wv

            wch_free = [0, 0]
            gate_free = [0, 0]
            up_free = [0, 0]
            vh_last = 0
            hq_ps_free = [0, 0]
            hq_ps_ctr = [0]
            ht_ps_free = [0, 0]
            ht_ctr = [0]
            hT_free = [0, 0]
            hst_gate = [[0] * 4 for _ in range(2)]
            hst_ev = [[0] * 4 for _ in range(2)]
            hqs_wr = [0]
            grp_free = 0
            iter_ctr = 0
            strip_seq = [0]
            ready = []
            pend = []

            def hq_tp4(tiles, idap, gate, bf=True):
                if bf:
                    r = bf_ctr[0] % 2
                    bf_ctr[0] += 1
                    psv = tbf[r][:, 0:512]
                    free = bf_free
                else:
                    r = ht_ctr[0] % 2
                    ht_ctr[0] += 1
                    psv = htf[r][:, 0:256]
                    free = ht_ps_free
                w0 = list(gate) + [(sS, sI), (sV, free[r])]
                tmm = 0
                for k4, tile in enumerate(tiles):
                    tmm = onT(lambda e, t_=tile, k4=k4, psv=psv: e.transpose(
                        out=psv[:, k4 * 128:(k4 + 1) * 128],
                        in_=t_, identity=idap), w0)
                    w0 = ()
                return tmm, psv, r

            def flush_tp(ent):
                g4d, icd, tt4d, shi, vlo, _ = ent
                qtm, psv, r = hq_tp4(
                    [q_hi[:, k * 128:(k + 1) * 128] for k in range(4)],
                    ident[:], [(sS, shi), (wHQT, hqs_wr[0])])
                evh = onV(lambda e, psv=psv: e.tensor_copy(
                    out=hqs_hi[:], in_=psv[:, 0:512]), [(sT, qtm)])
                bf_free[r] = evh
                qtm, psv, r = hq_tp4(
                    [q_lo[:, k * 128:(k + 1) * 128] for k in range(4)],
                    ident[:], [(sV, vlo), (wHQT, hqs_wr[0])])
                evl = onV(lambda e, psv=psv: e.tensor_copy(
                    out=hqs_lo[:], in_=psv[:, 0:512]), [(sT, qtm)])
                bf_free[r] = evl
                qws["w"] = [(sT, qtm)]
                dma_wr(wHQT, d_hqT_hi[g4d * 4 + tt4d, icd, :, :],
                       hqs_hi[:], [(sV, evh)])
                hqs_wr[0] = dma_wr(wHQT, d_hqT_lo[g4d * 4 + tt4d, icd, :, :],
                                   hqs_lo[:], [(sV, evl)])

            def flush_pend():
                while pend and pend[0][5] < strip_seq[0]:
                    flush_tp(pend.pop(0))

            def emit_one_chain():
                if ready:
                    g4d, icd, tt4d, buf = ready.pop(0)
                    shi, vlo = emit_quant(hstage[buf][tt4d][:],
                                          prep_env["ws"],
                                          [(sV, hst_ev[buf][tt4d])])
                    hst_gate[buf][tt4d] = shi
                    pend.append((g4d, icd, tt4d, shi, vlo, strip_seq[0]))
                strip_seq[0] += 1

            for g4 in range(4):
                lw = [(sT, grp_free)]
                for tt4 in range(4):
                    for hf in range(4):
                        dma_in(rXQG,
                               xqg_w_hi[:, hf, :, tt4 * 128:(tt4 + 1) * 128],
                               d_xqT_hi[g4 * 4 + tt4, hf, :, :].rearrange(
                                   "p (c t) -> p c t", t=128), lw)
                        lw = ()
                        dma_in(rXQG,
                               xqg_w_lo[:, hf, :, tt4 * 128:(tt4 + 1) * 128],
                               d_xqT_lo[g4 * 4 + tt4, hf, :, :].rearrange(
                                   "p (c t) -> p c t", t=128))
                xq_ready = rXQG.n
                mm_l = 0
                for istrip in range(64):
                    pb = istrip % 2
                    lw = [(sT, wch_free[pb])]
                    for j, dsp in enumerate((d_gwT_hi[istrip], d_gwT_lo[istrip],
                                             d_uwT_hi[istrip],
                                             d_uwT_lo[istrip])):
                        dma_in(rGWS[pb], ws_w[pb][j][:], dsp, lw)
                        lw = ()
                    wready = rGWS[pb].n
                    for half, (w_h, w_l) in enumerate(
                            ((ws_w[pb][0], ws_w[pb][1]),
                             (ws_w[pb][2], ws_w[pb][3]))):
                        bk = pb if half == 0 else 2 + pb
                        wq = [(rGWS[pb], wready), (rXQG, xq_ready)]
                        wq.append((sS, gate_free[pb]) if half == 0
                                  else (sV, up_free[pb]))
                        for hc in range(16):
                            for pi, (wt, xf) in enumerate(
                                    ((w_h, xqg_hi), (w_h, xqg_lo),
                                     (w_l, xqg_hi))):
                                mm_l = onT(lambda e, wt=wt, xf=xf, hc=hc,
                                           bk=bk, st=(hc == 0 and pi == 0),
                                           sp=(hc == 15 and pi == 2): e.matmul(
                                    out=P_A[:, bk * 512:(bk + 1) * 512],
                                    lhsT=wt[:, hc * 128:(hc + 1) * 128],
                                    rhs=xf[:, hc * 512:(hc + 1) * 512],
                                    start=st, stop=sp), wq)
                                wq = ()
                    wch_free[pb] = mm_l
                    sl = istrip % 2
                    sg = onS(lambda e, pb=pb: e.activation(
                        out=gtmp[:], in_=P_A[:, pb * 512:(pb + 1) * 512],
                        func=Act.Silu), [(sT, mm_l), (sV, vh_last)])
                    gate_free[pb] = sg
                    vh = onV(lambda e, pb=pb, sl=sl: e.tensor_tensor(
                        out=hT[sl][:], in0=gtmp[:],
                        in1=P_A[:, (2 + pb) * 512:(3 + pb) * 512],
                        op=Alu.mult), [(sS, sg), (sT, hT_free[sl])])
                    up_free[pb] = vh
                    vh_last = vh
                    i4 = istrip % 4
                    ic = istrip // 4
                    buf = ic % 2
                    tmm = 0
                    ev = 0
                    for tt4 in range(4):
                        tmm, psv, r = hq_tp4(
                            [hT[sl][:, tt4 * 128:(tt4 + 1) * 128]],
                            identF[:], [(sV, vh)], bf=False)
                        ev = onV(lambda e, tt4=tt4, i4=i4, buf=buf, psv=psv:
                                 e.tensor_copy(
                            out=hstage[buf][tt4][:, i4 * 128:(i4 + 1) * 128],
                            in_=psv[:, 0:128]),
                            [(sT, tmm)]
                            + ([(sS, hst_gate[buf][tt4])] if i4 == 0 else []))
                        ht_ps_free[r] = ev
                        hst_ev[buf][tt4] = ev
                    hT_free[sl] = tmm
                    if i4 == 3:
                        for tt4 in range(4):
                            ready.append((g4, ic, tt4, buf))
                    flush_pend()
                    if iter_ctr < 64:
                        emit_prep_steps(1)
                    elif iter_ctr < 192:
                        emit_dml_steps(1)
                    iter_ctr += 1
                    emit_one_chain()
                grp_free = mm_l
            while ready or pend:
                flush_pend()
                emit_one_chain()
            emit_prep_steps(len(prep_jobs))
            emit_dml_steps(len(dml_jobs))

        barrier()

        with contextlib.ExitStack() as ph_d:
            dwc_hi = [sb(f"dwch{i}", [128, 2048], BF16, ph_d) for i in range(2)]
            dwc_lo = [sb(f"dwcl{i}", [128, 2048], BF16, ph_d) for i in range(2)]
            hq_h = [sb(f"hqh{i}", [128, 4096], BF16, ph_d) for i in range(2)]
            hq_l = [sb(f"hql{i}", [128, 4096], BF16, ph_d) for i in range(2)]
            gsl = sb("gsl", [128, 512], F32, ph_d)
            osl = sb("osl", [128, 512], F32, ph_d)
            ssl = [sb(f"ssl{i}", [128, 512], F32, ph_d) for i in range(2)]
            xsl = [sb(f"xsl{i}", [128, 512], F32, ph_d) for i in range(2)]

            pd2 = ph_d.enter_context(
                nc.psum_tensor("PD2", [128, 2048], F32))

            rDW = [stream("rDW0"), stream("rDW1")]
            rXD = [stream("rXD0"), stream("rXD1")]
            wOUT = stream("wOUT")

            def PPbank(t8):
                PP = P_A if t8 < 4 else pd2
                return PP[:, (t8 % 4) * 512:((t8 % 4) + 1) * 512]

            dw_free = [0, 0]
            bank_sfree = [0] * 8
            out_wr = 0
            vo1_hist = [0, 0]
            vo2_hist = [0, 0]

            def emit_xs_loads(tkg, hg, t8):
                tt = tkg * 8 + t8
                sl = t8 % 2
                lx = dma_in(rXD[sl], xsl[sl][:],
                            d_x[tt * 128:(tt + 1) * 128,
                                hg * 512:(hg + 1) * 512], [(sV, vo2_hist[sl])])
                ls = dma_in(rXD[sl], ssl[sl][:], d_sattnB[tt, hg, :, :],
                            [(sV, vo1_hist[sl])])
                return lx, ls, sl

            for tkg in range(2):
                for hg in range(4):
                    mm_l = 0
                    for ic in range(16):
                        cb = ic % 2
                        lw = [(sT, dw_free[cb])]
                        for isub in range(4):
                            istrip = ic * 4 + isub
                            for ht in range(4):
                                dma_in(rDW[cb],
                                       dwc_hi[cb][:, isub * 512 + ht * 128:
                                                  isub * 512 + (ht + 1) * 128],
                                       d_dwT_hi[istrip, hg * 4 + ht, :, :], lw)
                                lw = ()
                                dma_in(rDW[cb],
                                       dwc_lo[cb][:, isub * 512 + ht * 128:
                                                  isub * 512 + (ht + 1) * 128],
                                       d_dwT_lo[istrip, hg * 4 + ht, :, :])
                        for t8 in range(8):
                            dma_in(rDW[cb],
                                   hq_h[cb][:, t8 * 512:(t8 + 1) * 512],
                                   d_hqT_hi[tkg * 8 + t8, ic, :, :])
                            dma_in(rDW[cb],
                                   hq_l[cb][:, t8 * 512:(t8 + 1) * 512],
                                   d_hqT_lo[tkg * 8 + t8, ic, :, :])
                        dready = rDW[cb].n
                        first = True
                        for t8 in range(8):
                            for isub in range(4):
                                for pi, (hh_, dd_) in enumerate(
                                        ((hq_h[cb], dwc_hi[cb]),
                                         (hq_h[cb], dwc_lo[cb]),
                                         (hq_l[cb], dwc_hi[cb]))):
                                    st_f = (ic == 0 and isub == 0 and pi == 0)
                                    wq = []
                                    if first:
                                        wq.append((rDW[cb], dready))
                                        first = False
                                    if st_f:
                                        wq.append((sS, bank_sfree[t8]))
                                    mm_l = onT(lambda e, hh_=hh_, dd_=dd_,
                                               isub=isub, t8=t8, st=st_f,
                                               sp=(ic == 15 and isub == 3
                                                   and pi == 2): e.matmul(
                                        out=PPbank(t8),
                                        lhsT=hh_[:, t8 * 512 + isub * 128:
                                                 t8 * 512 + (isub + 1) * 128],
                                        rhs=dd_[:, isub * 512:
                                                (isub + 1) * 512],
                                        start=st, stop=sp), wq)
                        dw_free[cb] = mm_l
                    lx, ls, lsl = emit_xs_loads(tkg, hg, 0)
                    for t8 in range(8):
                        tt = tkg * 8 + t8
                        sl = t8 % 2
                        nlx = nls = 0
                        if t8 < 7:
                            nlx, nls, _ = emit_xs_loads(tkg, hg, t8 + 1)
                        sg_d = onS(lambda e, t8=t8: e.activation(
                            out=gsl[:], in_=PPbank(t8), func=Act.Silu),
                            [(sT, mm_l), (sV, vo1_hist[0]),
                             (sV, vo1_hist[1])])
                        bank_sfree[t8] = sg_d
                        vo1 = onV(lambda e, sl=sl: e.tensor_tensor(
                            out=osl[:], in0=gsl[:], in1=ssl[sl][:],
                            op=Alu.mult),
                            [(sS, sg_d), (rXD[sl], ls), (wOUT, out_wr)])
                        vo2 = onV(lambda e, sl=sl: e.tensor_tensor(
                            out=osl[:], in0=osl[:], in1=xsl[sl][:],
                            op=Alu.add), [(rXD[sl], lx)])
                        out_wr = dma_wr(wOUT,
                                        d_out[tt * 128:(tt + 1) * 128,
                                              hg * 512:(hg + 1) * 512],
                                        osl[:], [(sV, vo2)])
                        vo1_hist[sl] = vo1
                        vo2_hist[sl] = vo2
                        if t8 < 7:
                            lx, ls = nlx, nls

        P.on("sync", None, [(s_, s_.n) for s_ in _streams.values()])

        with contextlib.ExitStack() as sem_stack:
            for s_ in all_sems:
                s_.h = sem_stack.enter_context(nc.semaphore(s_.name))
            block = sem_stack.enter_context(nc.Block())

            def run_queue(eng_name):
                def body(eng):
                    for (fn, waits, inc, inc_n) in P.q[eng_name]:
                        for (s_, v) in waits:
                            eng.wait_ge(s_.h, v)
                        if fn is not None:
                            ins = fn(eng)
                            if inc is not None:
                                ins.then_inc(inc.h, inc_n)
                        elif inc is not None:
                            eng.nop().then_inc(inc.h, inc_n)
                return body

            block.sync(run_queue("sync"))
            block.vector(run_queue("vector"))
            block.scalar(run_queue("scalar"))
            block.tensor(run_queue("tensor"))
            block.gpsimd(run_queue("gpsimd"))

    return nc


_NC_CACHE = {}


def kernel(L0_attn_x, L0_attn_w_qkv, L0_attn_w_o, L0_ffn_gate_w, L0_ffn_up_w,
           L0_ffn_down_w):
    x = np.ascontiguousarray(np.asarray(L0_attn_x, dtype=np.float32))
    wqkv = np.ascontiguousarray(np.asarray(L0_attn_w_qkv, dtype=np.float32))
    wo = np.ascontiguousarray(np.asarray(L0_attn_w_o, dtype=np.float32))
    gw = np.ascontiguousarray(np.asarray(L0_ffn_gate_w, dtype=np.float32))
    uw = np.ascontiguousarray(np.asarray(L0_ffn_up_w, dtype=np.float32))
    dw = np.ascontiguousarray(np.asarray(L0_ffn_down_w, dtype=np.float32))
    cst = np.tile(np.array([CST_VALS], np.float32), (128, 1))
    ones1 = np.ones((1, 128), np.float32)
    idf = np.eye(128, dtype=np.float32)

    if "nc" not in _NC_CACHE:
        _NC_CACHE["nc"] = build_program()
    nc = _NC_CACHE["nc"]
    in_maps = []
    for c in range(8):
        in_maps.append({"x": x[c], "wqkv": wqkv, "wo": wo, "gw": gw, "uw": uw,
                        "dw": dw, "cst": cst, "ones1": ones1, "identf": idf})
    res = run_bass_kernel_spmd(nc, in_maps, core_ids=list(range(8)))
    return np.stack([res.results[c]["out"] for c in range(8)], axis=0)


# version 08
# speedup vs baseline: 1.0260x; 1.0160x over previous
import contextlib
import numpy as np
import concourse.bass as bass
import concourse.mybir as mybir
from concourse.bass_utils import run_bass_kernel_spmd

F32 = mybir.dt.float32
BF16 = mybir.dt.bfloat16
I32 = mybir.dt.int32
Alu = mybir.AluOpType
Act = mybir.ActivationFunctionType
AX = mybir.AxisListType

S = 2048
H = 2048
NH = 16
FI = 8192
NTT = S // 128
QL = 512

_D = np.sqrt(np.float32(128.0)).astype(np.float32)
_INVD = 1.0 / np.float64(_D)
CW_HI = float(np.float32(_INVD))
CW_LO = float(np.float32(np.float64(_INVD) - np.float64(np.float32(_INVD))))
EPS_LN = float(np.float32(1e-5))
INV_H = 1.0 / 2048.0

C_448, C_NEG8 = 0, 1
CST_VALS = [448.0, -8.0]


class Sem:
    def __init__(self, name):
        self.name = name
        self.h = None
        self.n = 0

    def inc(self, k):
        self.n += k
        return self.n


class Prog:
    ENGINES = ("sync", "vector", "scalar", "tensor", "gpsimd")

    def __init__(self):
        self.q = {e: [] for e in self.ENGINES}

    def on(self, eng, fn, waits=(), inc=None, inc_n=1):
        w = [(s, v) for (s, v) in waits if s is not None and v > 0]
        self.q[eng].append((fn, w, inc, inc_n))
        return inc.inc(inc_n) if inc is not None else None


DEBUG_SPILLS = ()
SAFE_V = True
SAFE_S = False
SAFE_DMA = False


def build_program():
    nc = bass.Bass("TRN2", target_bir_lowering=False, debug=False,
                   detect_race_conditions=False)
    P = Prog()

    def _internal(name, shape, dtype):
        kind = "ExternalOutput" if name in DEBUG_SPILLS else "Internal"
        return nc.dram_tensor(name, shape, dtype, kind=kind).ap()

    d_x = nc.dram_tensor("x", [S, H], F32, kind="ExternalInput").ap()
    d_wqkv = nc.dram_tensor("wqkv", [H, 3 * H], F32, kind="ExternalInput").ap()
    d_wo = nc.dram_tensor("wo", [H, H], F32, kind="ExternalInput").ap()
    d_gw = nc.dram_tensor("gw", [FI, H], F32, kind="ExternalInput").ap()
    d_uw = nc.dram_tensor("uw", [FI, H], F32, kind="ExternalInput").ap()
    d_dw = nc.dram_tensor("dw", [H, FI], F32, kind="ExternalInput").ap()
    d_cst = nc.dram_tensor("cst", [128, len(CST_VALS)], F32, kind="ExternalInput").ap()
    d_ones = nc.dram_tensor("ones1", [1, 128], F32, kind="ExternalInput").ap()
    d_idf = nc.dram_tensor("identf", [128, 128], F32, kind="ExternalInput").ap()
    d_out = nc.dram_tensor("out", [S, H], F32, kind="ExternalOutput").ap()

    d_wqkvB = _internal("wqkvB", [48, 16, 128, 128], F32)
    d_swo_hi = _internal("swo_hi", [16, 4, 128, 512], BF16)
    d_swo_lo = _internal("swo_lo", [16, 4, 128, 512], BF16)
    d_sgw_hi = _internal("sgw_hi", [64, 4, 128, 512], BF16)
    d_sgw_lo = _internal("sgw_lo", [64, 4, 128, 512], BF16)
    d_suw_hi = _internal("suw_hi", [64, 4, 128, 512], BF16)
    d_suw_lo = _internal("suw_lo", [64, 4, 128, 512], BF16)
    d_sdw_hi = _internal("sdw_hi", [16, 16, 128, 512], BF16)
    d_sdw_lo = _internal("sdw_lo", [16, 16, 128, 512], BF16)
    d_gwT_hi = _internal("gwT_hi", [64, 128, 2048], BF16)
    d_gwT_lo = _internal("gwT_lo", [64, 128, 2048], BF16)
    d_uwT_hi = _internal("uwT_hi", [64, 128, 2048], BF16)
    d_uwT_lo = _internal("uwT_lo", [64, 128, 2048], BF16)
    d_dwT_hi = _internal("dwT_hi", [64, 16, 128, 128], BF16)
    d_dwT_lo = _internal("dwT_lo", [64, 16, 128, 128], BF16)
    d_xqT_hi = _internal("xqT_hi", [NTT, 4, 128, 512], BF16)
    d_xqT_lo = _internal("xqT_lo", [NTT, 4, 128, 512], BF16)
    d_hqT_hi = _internal("hqT_hi", [NTT, 16, 128, 512], BF16)
    d_hqT_lo = _internal("hqT_lo", [NTT, 16, 128, 512], BF16)
    d_sval_hi = _internal("sval_hi", [NTT, NH, 128, 128], BF16)
    d_sval_lo = _internal("sval_lo", [NTT, NH, 128, 128], BF16)
    d_sattnB = _internal("sattnB", [NTT, 4, 128, 512], F32)

    sT, sV, sS, ph = (Sem(n) for n in ("sT", "sV", "sS", "ph"))
    all_sems = [sT, sV, sS, ph]
    _streams = {}

    def stream(name):
        if name not in _streams:
            s_ = Sem(name)
            _streams[name] = s_
            all_sems.append(s_)
        return _streams[name]

    def dma_in(sem, out_ap, in_ap, waits=(), q="sync"):
        return P.on(q, lambda e, o=out_ap, i=in_ap: e.dma_start(out=o, in_=i),
                    waits, sem, 16)

    def dma_wr(sem, out_ap, in_ap, waits=()):
        return P.on("gpsimd", lambda e, o=out_ap, i=in_ap:
                    e.dma_start(out=o, in_=i), waits, sem, 16)

    def onV(fn, waits=()):
        w = list(waits)
        if SAFE_V:
            w.append((sV, sV.n))
        return P.on("vector", fn, w, sV, 1)

    def onS(fn, waits=()):
        w = list(waits)
        if SAFE_S:
            w.append((sS, sS.n))
        return P.on("scalar", fn, w, sS, 1)

    def onT(fn, waits=()):
        return P.on("tensor", fn, waits, sT, 1)

    def barrier():
        dw_ = [(s_, s_.n) for s_ in _streams.values()]
        for e in Prog.ENGINES:
            P.on(e, lambda en: en.nop(), dw_, ph, 1)
        tgt = ph.n
        for e in Prog.ENGINES:
            P.on(e, None, [(ph, tgt)])

    outer = contextlib.ExitStack()
    with outer:
        def sb(name, shape, dtype=F32, st=None):
            return (st or outer).enter_context(nc.sbuf_tensor(name, shape, dtype))

        cst_t = sb("cst_t", [128, len(CST_VALS)])
        r448 = sb("r448", [128, 1])
        ones_t = sb("ones_t", [1, 128])
        identF = sb("identF", [128, 128], F32)
        ident = sb("ident", [128, 128], BF16)
        q_sp = sb("q_sp", [128, 1])
        q_ya = sb("q_ya", [128, QL])
        q_v = sb("q_v", [128, QL])
        q_eb = sb("q_eb", [128, QL], I32)
        q_ib = sb("q_ib", [128, QL], I32)
        q_qb = sb("q_qb", [128, QL], I32)
        q_hi = sb("q_hi", [128, QL], BF16)
        q_lo = sb("q_lo", [128, QL], BF16)
        q_m = sb("q_m", [128, QL // 32])
        q_s = sb("q_s", [128, QL // 32])
        q_rs = sb("q_rs", [128, QL // 32])

        rCST = stream("rCST")
        dma_in(rCST, cst_t[:], d_cst)
        dma_in(rCST, ones_t[:], d_ones)
        id_rdy = dma_in(rCST, identF[:], d_idf)
        sI = onS(lambda e: e.copy(out=ident[:], in_=identF[:]), [(rCST, id_rdy)])
        onV(lambda e: e.tensor_scalar(
            out=q_sp[:], in0=cst_t[:, C_448:C_448 + 1], scalar1=1.0, scalar2=None,
            op0=Alu.mult), [(rCST, id_rdy)])
        r448_rdy = onV(lambda e: e.reciprocal(out=r448[:],
                                              in_=cst_t[:, C_448:C_448 + 1]))

        P_A = outer.enter_context(nc.psum_tensor("P_A", [128, 2048], F32))

        rPREP = stream("rPREP")
        wQ = stream("wQ")
        qws = {"w": []}

        def emit_quant(src_ap, ws, src_waits=()):
            L = QL
            nb = L // 32
            w_src = list(src_waits)
            qt_, qu_, qf_ = ws["t"], ws["u"], ws["f"]
            sya = onS(lambda e, a=src_ap: e.activation(
                out=q_ya[:], in_=a, func=Act.Abs), w_src + [(sV, sV.n)])
            onV(lambda e, a=src_ap: e.tensor_reduce(
                out=q_m[:], in_=a.rearrange("p (b k) -> p b k", k=32),
                axis=AX.X, op=Alu.max, apply_absolute_value=True), w_src)
            onV(lambda e: e.tensor_scalar(
                out=q_s[:], in0=q_m[:], scalar1=r448[:], scalar2=1e-30,
                op0=Alu.mult, op1=Alu.max), [(sV, r448_rdy)])
            onV(lambda e: e.tensor_scalar(out=q_sp[:], in0=q_s[:, 0:1],
                                          scalar1=1.0, scalar2=None,
                                          op0=Alu.mult))
            vrs = onV(lambda e: e.reciprocal(out=q_rs[:], in_=q_s[:]))
            vt = onV(lambda e, qt_=qt_: e.tensor_tensor(
                out=qt_.rearrange("p (b k) -> p b k", k=32),
                in0=q_ya[:].rearrange("p (b k) -> p b k", k=32),
                in1=q_rs[:].unsqueeze(2).broadcast_to([128, nb, 32]),
                op=Alu.mult), [(sS, sya), (sV, vrs)])
            onV(lambda e, qt_=qt_: e.tensor_scalar(
                out=q_eb[:], in0=qt_.bitcast(I32), scalar1=23, scalar2=None,
                op0=Alu.logical_shift_right))
            onV(lambda e: e.tensor_scalar(
                out=q_eb[:], in0=q_eb[:], scalar1=121.0, scalar2=None,
                op0=Alu.max))
            onV(lambda e: e.tensor_scalar(
                out=q_ib[:], in0=q_eb[:], scalar1=-1.0, scalar2=257.0,
                op0=Alu.mult, op1=Alu.add))
            onV(lambda e: e.tensor_scalar(
                out=q_ib[:], in0=q_ib[:], scalar1=23, scalar2=None,
                op0=Alu.logical_shift_left))
            onV(lambda e: e.tensor_scalar(
                out=q_qb[:], in0=q_eb[:], scalar1=3.0, scalar2=None,
                op0=Alu.subtract))
            onV(lambda e: e.tensor_scalar(
                out=q_qb[:], in0=q_qb[:], scalar1=23, scalar2=None,
                op0=Alu.logical_shift_left))
            onV(lambda e, qt_=qt_, qu_=qu_: e.tensor_tensor(
                out=qu_, in0=qt_, in1=q_ib[:].bitcast(F32), op=Alu.mult))
            onV(lambda e, qu_=qu_: e.tensor_copy(out=q_eb[:], in_=qu_))
            onV(lambda e, qu_=qu_, qf_=qf_: e.tensor_tensor(
                out=qf_, in0=qu_, in1=q_eb[:], op=Alu.subtract))
            onV(lambda e, qf_=qf_: e.scalar_tensor_tensor(
                out=q_v[:], in0=qf_, scalar=0.5, in1=q_eb[:],
                op0=Alu.is_ge, op1=Alu.add))
            onV(lambda e, qu_=qu_: e.tensor_tensor(
                out=qu_, in0=q_v[:], in1=q_qb[:].bitcast(F32), op=Alu.mult))
            onV(lambda e, qt_=qt_, qu_=qu_: e.tensor_tensor(
                out=qt_.rearrange("p (b k) -> p b k", k=32),
                in0=qu_.rearrange("p (b k) -> p b k", k=32),
                in1=q_s[:].unsqueeze(2).broadcast_to([128, nb, 32]),
                op=Alu.mult))
            ssg = onS(lambda e, a=src_ap: e.activation(
                out=q_ya[:], in_=a, func=Act.Sign), [(sV, vt)])
            vdq = onV(lambda e, qt_=qt_, qf_=qf_: e.tensor_tensor(
                out=qf_, in0=qt_, in1=q_ya[:], op=Alu.mult), [(sS, ssg)])
            shi = onS(lambda e, qf_=qf_: e.copy(out=q_hi[:], in_=qf_),
                      [(sV, vdq)] + list(qws["w"]))
            vlo = onV(lambda e, qf_=qf_: e.tensor_tensor(
                out=q_lo[:], in0=qf_, in1=q_hi[:], op=Alu.subtract),
                [(sS, shi)])
            return shi, vlo

        prep_jobs = ([("wo", i) for i in range(16)]
                     + [("gw", i) for i in range(64)]
                     + [("uw", i) for i in range(64)]
                     + [("dw", i) for i in range(64)])
        prep_state = {"i": 0, "s_done": 0, "wr": {}}
        prep_env = {}

        def emit_prep_steps(k):
            for _ in range(k):
                i = prep_state["i"]
                if i >= len(prep_jobs):
                    return
                prep_state["i"] = i + 1
                tag, idx = prep_jobs[i]
                strip = prep_env["strip"]
                if tag == "wo":
                    src = d_wo[idx * 128:(idx + 1) * 128, :]
                    blks = [(d_swo_hi[idx, b], d_swo_lo[idx, b]) for b in range(4)]
                elif tag == "gw":
                    src = d_gw[idx * 128:(idx + 1) * 128, :]
                    blks = [(d_sgw_hi[idx, b], d_sgw_lo[idx, b]) for b in range(4)]
                elif tag == "uw":
                    src = d_uw[idx * 128:(idx + 1) * 128, :]
                    blks = [(d_suw_hi[idx, b], d_suw_lo[idx, b]) for b in range(4)]
                else:
                    s_i, c4 = idx // 4, idx % 4
                    src = d_dw[s_i * 128:(s_i + 1) * 128,
                               c4 * 2048:(c4 + 1) * 2048]
                    blks = [(d_sdw_hi[s_i, c4 * 4 + b], d_sdw_lo[s_i, c4 * 4 + b])
                            for b in range(4)]
                lw = list(qws["w"]) + prep_env["ld_extra"]()
                ld = dma_in(rPREP, strip[:], src, lw, q="gpsimd")
                shi = 0
                for b in range(4):
                    sap = strip[:, b * 512:(b + 1) * 512]
                    if tag == "wo":
                        shi = onS(lambda e, a=sap: e.copy(out=q_hi[:], in_=a),
                                  [(rPREP, ld), (sV, sV.n)] + list(qws["w"]))
                        vlo = onV(lambda e, a=sap: e.tensor_tensor(
                            out=q_lo[:], in0=a, in1=q_hi[:], op=Alu.subtract),
                            [(sS, shi)])
                    else:
                        shi, vlo = emit_quant(sap, prep_env["ws"], [(rPREP, ld)])
                    dma_wr(wQ, blks[b][0][:, :], q_hi[:], [(sS, shi)])
                    w2 = dma_wr(wQ, blks[b][1][:, :], q_lo[:], [(sV, vlo)])
                    qws["w"] = [(wQ, w2)]
                prep_state["s_done"] = shi
                prep_state["wr"][(tag, idx)] = w2

        rWB = stream("rWB")
        wWB = stream("wWB")
        rXT = stream("rXT")
        wXQT = stream("wXQT")

        with contextlib.ExitStack() as ph_aa:
            xn_fm_hi = sb("xn_fm_hi", [128, 16 * 2048], BF16, ph_aa)
            xn_fm_lo = sb("xn_fm_lo", [128, 16 * 2048], BF16, ph_aa)
            fm_hi_r = xn_fm_hi[:].rearrange("p (c t) -> p c t", t=2048)
            fm_lo_r = xn_fm_lo[:].rearrange("p (c t) -> p c t", t=2048)

            with contextlib.ExitStack() as ph_a0:
                xt = sb("xt", [128, 2048], F32, ph_a0)
                xn = sb("xn", [128, 2048], F32, ph_a0)
                xnhi = sb("xnhi", [128, 2048], BF16, ph_a0)
                xnlo = sb("xnlo", [128, 2048], BF16, ph_a0)
                xqs_hi = sb("xqs_hi", [128, 512], BF16, ph_a0)
                xqs_lo = sb("xqs_lo", [128, 512], BF16, ph_a0)
                wb_stg = sb("wb_stg", [128, 3072], F32, ph_a0)
                a0_t = sb("a0_t", [128, QL], F32, ph_a0)
                a0_u = sb("a0_u", [128, QL], F32, ph_a0)
                a0_f = sb("a0_f", [128, QL], F32, ph_a0)
                a0_strip = sb("a0_strip", [128, 2048], F32, ph_a0)
                lnsum = sb("lnsum", [128, 1], F32, ph_a0)
                lnm = sb("lnm", [128, 1], F32, ph_a0)
                lnnm = sb("lnnm", [128, 1], F32, ph_a0)
                lnvs = sb("lnvs", [128, 1], F32, ph_a0)
                lnsd = sb("lnsd", [128, 1], F32, ph_a0)
                lnr = sb("lnr", [128, 1], F32, ph_a0)
                lnb = sb("lnb", [128, 1], F32, ph_a0)
                tpb = [ph_a0.enter_context(
                    nc.psum_tensor(f"TPB{i}", [128, 1024], BF16))
                    for i in range(4)]

                prep_env["strip"] = a0_strip
                prep_env["ws"] = {"t": a0_t[:], "u": a0_u[:], "f": a0_f[:]}
                prep_env["ld_extra"] = lambda: []

                tp_free = [0, 0, 0, 0]
                tp_ctr = [0]
                xqs_wr = [0]
                xt_free = {"v": 0, "s": 0}
                xn_free_v = 0
                xnsplit_t = 0

                def tp_batch4(src_tiles, evac_fn, gate, ev_extra=()):
                    bk = tp_ctr[0] % 4
                    tp_ctr[0] += 1
                    psb = tpb[bk][:, 0:512]
                    w0 = list(gate) + [(sS, sI), (sS, tp_free[bk])]
                    tmm = 0
                    for k4, tile in enumerate(src_tiles):
                        tmm = onT(lambda e, t_=tile, k4=k4, psb=psb:
                                  e.transpose(
                            out=psb[:, k4 * 128:(k4 + 1) * 128],
                            in_=t_, identity=ident[:]), w0)
                        w0 = ()
                    ev = onS(lambda e, psb=psb, f=evac_fn: f(
                        e, psb[:, 0:512]), [(sT, tmm)] + list(ev_extra))
                    tp_free[bk] = ev
                    return tmm, ev

                for tt in range(NTT):
                    ld = dma_in(rXT, xt[:], d_x[tt * 128:(tt + 1) * 128, :],
                                [(sV, xt_free["v"]), (sS, xt_free["s"])])
                    for half in range(2):
                        lwb = dma_in(rWB, wb_stg[:],
                                     d_wqkv[tt * 128:(tt + 1) * 128,
                                            half * 3072:(half + 1) * 3072],
                                     [(wWB, wWB.n)], q="gpsimd")
                        for k in range(24):
                            fi = half * 24 + k
                            dma_wr(wWB, d_wqkvB[fi, tt, :, :],
                                   wb_stg[:, k * 128:(k + 1) * 128],
                                   [(rWB, lwb)] if k == 0 else ())
                    onV(lambda e: e.reduce_sum(out=lnsum[:], in_=xt[:],
                                               axis=AX.X), [(rXT, ld)])
                    onV(lambda e: e.tensor_scalar(out=lnm[:], in0=lnsum[:],
                                                  scalar1=INV_H, scalar2=None,
                                                  op0=Alu.mult))
                    vnm = onV(lambda e: e.tensor_scalar(
                        out=lnnm[:], in0=lnm[:], scalar1=-1.0, scalar2=None,
                        op0=Alu.mult))
                    ssq = onS(lambda e: e.activation(
                        out=P_A[:, 0:2048], in_=xt[:], func=Act.Square,
                        bias=lnnm[:], accum_out=lnvs[:]),
                        [(sV, vnm), (rXT, ld)])
                    vv1 = onV(lambda e: e.tensor_scalar(
                        out=lnvs[:], in0=lnvs[:], scalar1=INV_H, scalar2=EPS_LN,
                        op0=Alu.mult, op1=Alu.add), [(sS, ssq)])
                    ssd = onS(lambda e: e.activation(out=lnsd[:], in_=lnvs[:],
                                                     func=Act.Sqrt), [(sV, vv1)])
                    onV(lambda e: e.tensor_scalar(out=q_sp[:], in0=lnsd[:],
                                                  scalar1=1.0, scalar2=None,
                                                  op0=Alu.mult), [(sS, ssd)])
                    vr = onV(lambda e: e.reciprocal(out=lnr[:], in_=lnsd[:]))
                    vb = onV(lambda e: e.tensor_tensor(
                        out=lnb[:], in0=lnnm[:], in1=lnr[:], op=Alu.mult),
                        [(sV, vr)])
                    sxn = onS(lambda e: e.activation(
                        out=xn[:], in_=xt[:], func=Act.Identity, scale=lnr[:],
                        bias=lnb[:]), [(sV, vb), (sV, xn_free_v)])
                    sxh = onS(lambda e: e.copy(out=xnhi[:], in_=xn[:]),
                              [(sT, xnsplit_t)])
                    vxl = onV(lambda e: e.tensor_tensor(
                        out=xnlo[:], in0=xn[:], in1=xnhi[:], op=Alu.subtract),
                        [(sS, sxh), (sT, xnsplit_t)])
                    t_last = 0
                    for b4 in range(4):
                        hcs = [b4 * 4 + k for k in range(4)]
                        t_last, _ = tp_batch4(
                            [xnhi[:, hc * 128:(hc + 1) * 128] for hc in hcs],
                            lambda e, ps, b4=b4, tt=tt: e.copy(
                                out=fm_hi_r[:, b4 * 4:(b4 + 1) * 4,
                                            tt * 128:(tt + 1) * 128],
                                in_=ps.rearrange("p (c t) -> p c t", t=128)),
                            [(sV, vxl)])
                        t_last, _ = tp_batch4(
                            [xnlo[:, hc * 128:(hc + 1) * 128] for hc in hcs],
                            lambda e, ps, b4=b4, tt=tt: e.copy(
                                out=fm_lo_r[:, b4 * 4:(b4 + 1) * 4,
                                            tt * 128:(tt + 1) * 128],
                                in_=ps.rearrange("p (c t) -> p c t", t=128)),
                            [(sV, vxl)])
                    xnsplit_t = t_last
                    last_shi, last_vlo = 0, 0
                    for hf in range(4):
                        shi, vlo = emit_quant(xn[:, hf * 512:(hf + 1) * 512],
                                              prep_env["ws"], [(sS, sxn)])
                        qtm, evh = tp_batch4(
                            [q_hi[:, k * 128:(k + 1) * 128] for k in range(4)],
                            lambda e, ps: e.copy(out=xqs_hi[:], in_=ps),
                            [(sS, shi)], ev_extra=[(wXQT, xqs_wr[0])])
                        qtm, evl = tp_batch4(
                            [q_lo[:, k * 128:(k + 1) * 128] for k in range(4)],
                            lambda e, ps: e.copy(out=xqs_lo[:], in_=ps),
                            [(sV, vlo)], ev_extra=[(wXQT, xqs_wr[0])])
                        qws["w"] = [(sT, qtm)]
                        dma_wr(wXQT, d_xqT_hi[tt, hf, :, :], xqs_hi[:],
                               [(sS, evh)])
                        xqs_wr[0] = dma_wr(wXQT, d_xqT_lo[tt, hf, :, :],
                                           xqs_lo[:], [(sS, evl)])
                        last_shi, last_vlo = shi, vlo
                    emit_prep_steps(1)
                    xt_free = {"v": vxl, "s": last_shi}
                    xn_free_v = last_vlo

            barrier()

            with contextlib.ExitStack() as ph_a:
                wsl = sb("wsl", [128, 2048], F32, ph_a)
                whi = sb("whi", [128, 2048], BF16, ph_a)
                wlo = sb("wlo", [128, 2048], BF16, ph_a)
                Qhi = sb("Qhi", [128, 2048], BF16, ph_a)
                Qlo = sb("Qlo", [128, 2048], BF16, ph_a)
                Khi = sb("Khi", [128, 2048], BF16, ph_a)
                Klo = sb("Klo", [128, 2048], BF16, ph_a)
                Vfm = sb("Vfm", [128, 2048], F32, ph_a)
                E_t = [sb(f"E{i}", [128, 2048], F32, ph_a) for i in range(2)]
                valhi = sb("valhi", [128, 512], BF16, ph_a)
                vallo = sb("vallo", [128, 512], BF16, ph_a)
                rw4 = sb("rw4", [128, 8], F32, ph_a)
                rws = sb("rws", [128, 2], F32, ph_a)
                r_t = sb("r_t", [128, 2], F32, ph_a)
                c_ps = ph_a.enter_context(
                    nc.psum_tensor("PCL", [128, 2048], F32))

                c_row = E_t[0][0:1, :]
                prep_env["strip"] = E_t[0]
                prep_env["ws"] = {"t": Vfm[:, 0:512], "u": Vfm[:, 512:1024],
                                  "f": Vfm[:, 1024:1536]}
                prep_env["ld_extra"] = lambda: [(sT, sT.n)]

                rWSL = stream("rWSL")
                wVAL = stream("wVAL")
                wsl_free = {"v": 0, "s": 0}
                w_mm_free = 0
                qkv_ps = [(0, 0), (0, 0)]
                sc_ps = [(0, 0), (0, 0)]
                E_rd = [0, 0]
                rw_free = [0, 0]
                rt_busy = [0, 0]
                rt_ready = [0, 0]
                val_state = {"s": 0, "wr": 0}

                for hd in range(NH):
                    qk_gate = (0, 0)
                    for comp in range(3):
                        fi = comp * 16 + hd
                        lw0 = [(sS, wsl_free["s"]), (sV, wsl_free["v"])]
                        for hc in range(16):
                            dma_in(rWSL, wsl[:, hc * 128:(hc + 1) * 128],
                                   d_wqkvB[fi, hc, :, :], lw0)
                            lw0 = ()
                        wld = rWSL.n
                        if comp == 0:
                            vsc = 0
                            for p4 in range(4):
                                onV(lambda e, p4=p4: e.tensor_scalar(
                                    out=q_v[:],
                                    in0=wsl[:, p4 * 512:(p4 + 1) * 512],
                                    scalar1=CW_LO, scalar2=None, op0=Alu.mult),
                                    [(rWSL, wld)] if p4 == 0 else ())
                                vsc = onV(lambda e, p4=p4: e.scalar_tensor_tensor(
                                    out=wsl[:, p4 * 512:(p4 + 1) * 512],
                                    in0=wsl[:, p4 * 512:(p4 + 1) * 512],
                                    scalar=CW_HI, in1=q_v[:],
                                    op0=Alu.mult, op1=Alu.add))
                            wait_w = [(sV, vsc)]
                        else:
                            wait_w = [(rWSL, wld)]
                        swh = onS(lambda e: e.copy(out=whi[:], in_=wsl[:]),
                                  wait_w + [(sT, w_mm_free)])
                        vwl = onV(lambda e: e.tensor_tensor(
                            out=wlo[:], in0=wsl[:], in1=whi[:], op=Alu.subtract),
                            [(sS, swh), (sT, w_mm_free)])
                        wsl_free = {"v": vwl, "s": swh}
                        for tc in range(4):
                            qb = (comp * 4 + tc) % 2
                            wq = [(sV, vwl), (sV, qkv_ps[qb][0]),
                                  (sS, qkv_ps[qb][1])]
                            mm = 0
                            for hc in range(16):
                                for pi, (wt, xf) in enumerate(
                                        ((whi, fm_hi_r), (whi, fm_lo_r),
                                         (wlo, fm_hi_r))):
                                    mm = onT(lambda e, wt=wt, xf=xf, hc=hc,
                                             tc=tc, qb=qb,
                                             st=(hc == 0 and pi == 0),
                                             sp=(hc == 15 and pi == 2):
                                             e.matmul(
                                        out=P_A[:, qb * 512:(qb + 1) * 512],
                                        lhsT=wt[:, hc * 128:(hc + 1) * 128],
                                        rhs=xf[:, hc, tc * 512:(tc + 1) * 512],
                                        start=st, stop=sp), wq)
                                    wq = ()
                            w_mm_free = mm
                            if comp < 2:
                                dsth, dstl = ((Qhi, Qlo) if comp == 0
                                              else (Khi, Klo))
                                sh = onS(lambda e, d=dsth, tc=tc, qb=qb: e.copy(
                                    out=d[:, tc * 512:(tc + 1) * 512],
                                    in_=P_A[:, qb * 512:(qb + 1) * 512]),
                                    [(sT, mm)])
                                vl = onV(lambda e, dh=dsth, dl=dstl, tc=tc,
                                         qb=qb: e.tensor_tensor(
                                             out=dl[:, tc * 512:(tc + 1) * 512],
                                             in0=P_A[:, qb * 512:(qb + 1) * 512],
                                             in1=dh[:, tc * 512:(tc + 1) * 512],
                                             op=Alu.subtract), [(sS, sh)])
                                qkv_ps[qb] = (vl, sh)
                                if comp == 1:
                                    qk_gate = (vl, sh)
                            else:
                                vl = onV(lambda e, tc=tc, qb=qb: e.tensor_copy(
                                    out=Vfm[:, tc * 512:(tc + 1) * 512],
                                    in_=P_A[:, qb * 512:(qb + 1) * 512]),
                                    [(sT, mm), (sS, prep_state["s_done"])])
                                qkv_ps[qb] = (vl, qkv_ps[qb][1])
                        if comp < 2:
                            emit_prep_steps(2)
                    col_mm = [0]

                    def emit_colsum(qtc):
                        ebc = qtc % 2
                        w0 = [(sV, rt_ready[ebc])]
                        mmc = 0
                        for kc in range(4):
                            mmc = onT(lambda e, ebc=ebc, kc=kc, qtc=qtc:
                                      e.matmul(
                                out=c_ps[0:1, kc * 512:(kc + 1) * 512],
                                lhsT=r_t[:, ebc:ebc + 1],
                                rhs=E_t[ebc][:, kc * 512:(kc + 1) * 512],
                                start=(qtc == 0), stop=(qtc == 15)), w0)
                            w0 = ()
                        rt_busy[ebc] = mmc
                        E_rd[ebc] = mmc
                        col_mm[0] = mmc

                    se = 0
                    for qt in range(16):
                        eb = qt % 2
                        for kc in range(4):
                            b2 = kc % 2
                            w0 = [(sS, sc_ps[b2][1]), (sV, sc_ps[b2][0])]
                            if qt == 0 and kc == 0:
                                w0 += [(sV, qk_gate[0]), (sS, qk_gate[1])]
                            mm = 0
                            for pi, (qq, kk) in enumerate(
                                    ((Qhi, Khi), (Qhi, Klo), (Qlo, Khi))):
                                mm = onT(lambda e, qq=qq, kk=kk, qt=qt, kc=kc,
                                         b2=b2, st=(pi == 0), sp=(pi == 2):
                                         e.matmul(
                                    out=P_A[:, (2 + b2) * 512:(3 + b2) * 512],
                                    lhsT=qq[:, qt * 128:(qt + 1) * 128],
                                    rhs=kk[:, kc * 512:(kc + 1) * 512],
                                    start=st, stop=sp), w0)
                                w0 = ()
                            we = [(sT, mm)]
                            if kc == 0:
                                we += [(sV, rw_free[eb]), (sT, E_rd[eb])]
                                if qt == 0:
                                    we.append((rPREP, rPREP.n))
                            se = onS(lambda e, eb=eb, kc=kc, b2=b2:
                                     e.activation(
                                out=E_t[eb][:, kc * 512:(kc + 1) * 512],
                                in_=P_A[:, (2 + b2) * 512:(3 + b2) * 512],
                                func=Act.Exp, bias=cst_t[:, C_NEG8:C_NEG8 + 1],
                                accum_out=rw4[:, eb * 4 + kc:eb * 4 + kc + 1]),
                                we)
                            sc_ps[b2] = (sc_ps[b2][0], se)
                        vr1 = onV(lambda e, eb=eb: e.reduce_sum(
                            out=rws[:, eb:eb + 1],
                            in_=rw4[:, eb * 4:eb * 4 + 4], axis=AX.X),
                            [(sS, se)])
                        rw_free[eb] = vr1
                        onV(lambda e, eb=eb: e.tensor_scalar(
                            out=q_sp[:], in0=rws[:, eb:eb + 1], scalar1=1.0,
                            scalar2=None, op0=Alu.mult))
                        vrr = onV(lambda e, eb=eb: e.reciprocal(
                            out=r_t[:, eb:eb + 1], in_=rws[:, eb:eb + 1]),
                            [(sT, rt_busy[eb])])
                        rt_ready[eb] = vrr
                        if qt > 0:
                            emit_colsum(qt - 1)
                    emit_colsum(15)
                    vc = onV(lambda e: e.tensor_copy(out=c_row,
                                                     in_=c_ps[0:1, :]),
                             [(sT, col_mm[0])])
                    crep_last = 0
                    for kc in range(4):
                        b2 = kc % 2
                        crep_mm = onT(lambda e, kc=kc, b2=b2: e.matmul(
                            out=P_A[:, (2 + b2) * 512:(3 + b2) * 512],
                            lhsT=ones_t[:],
                            rhs=c_row[0:1, kc * 512:(kc + 1) * 512],
                            start=True, stop=True),
                            [(sV, vc), (sS, sc_ps[b2][1]), (sV, sc_ps[b2][0])])
                        crep_last = crep_mm
                        vv = onV(lambda e, kc=kc, b2=b2: e.tensor_tensor(
                            out=q_ya[:], in0=Vfm[:, kc * 512:(kc + 1) * 512],
                            in1=P_A[:, (2 + b2) * 512:(3 + b2) * 512],
                            op=Alu.mult),
                            [(sT, crep_mm), (sS, prep_state["s_done"]),
                             (sS, val_state["s"])])
                        svh = onS(lambda e: e.copy(out=valhi[:], in_=q_ya[:]),
                                  [(sV, vv), (wVAL, val_state["wr"])])
                        vvl = onV(lambda e: e.tensor_tensor(
                            out=vallo[:], in0=q_ya[:], in1=valhi[:],
                            op=Alu.subtract), [(sS, svh)])
                        wv = 0
                        for tq in range(4):
                            ti = kc * 4 + tq
                            dma_wr(wVAL, d_sval_hi[ti, hd, :, :],
                                   valhi[:, tq * 128:(tq + 1) * 128],
                                   [(sS, svh)] if tq == 0 else ())
                            wv = dma_wr(wVAL, d_sval_lo[ti, hd, :, :],
                                        vallo[:, tq * 128:(tq + 1) * 128],
                                        [(sV, vvl)] if tq == 0 else ())
                        val_state = {"s": svh, "wr": wv}
                        sc_ps[b2] = (vv, sc_ps[b2][1])
                    E_rd[0] = crep_last
                    emit_prep_steps(4)
                if prep_state["i"] < 144:
                    emit_prep_steps(144 - prep_state["i"])

        barrier()

        with contextlib.ExitStack() as ph_b:
            wo_hi = sb("wo_hi", [128, 16 * 1024], BF16, ph_b)
            wo_lo = sb("wo_lo", [128, 16 * 1024], BF16, ph_b)
            vt_hi = [sb(f"vt_hi{i}", [128, 2048], BF16, ph_b) for i in range(2)]
            vt_lo = [sb(f"vt_lo{i}", [128, 2048], BF16, ph_b) for i in range(2)]
            ao = sb("ao", [128, 1024], F32, ph_b)
            ml_in = [sb(f"ml_in{i}", [128, 2048], BF16, ph_b) for i in range(2)]
            ml_st = [sb(f"ml_st{i}", [128, 2048], BF16, ph_b) for i in range(2)]
            mlb = [ph_b.enter_context(
                nc.psum_tensor(f"MLB{i}", [128, 1024], BF16))
                for i in range(2)]

            rWO = stream("rWO")
            rVT = [stream("rVT0"), stream("rVT1")]
            wATTN = stream("wATTN")
            rMILL = [stream("rMILL0"), stream("rMILL1")]
            wMILL = [stream("wMILL0"), stream("wMILL1")]

            mill_jobs = [(srcb, dstb, s)
                         for (srcb, dstb) in ((d_sgw_hi, d_gwT_hi),
                                              (d_sgw_lo, d_gwT_lo),
                                              (d_suw_hi, d_uwT_hi),
                                              (d_suw_lo, d_uwT_lo))
                         for s in range(64)]
            mill_state = {"i": 0, "in_free": [0, 0], "st_free": [0, 0],
                          "ps_free": [0, 0]}

            def emit_mill_steps(k):
                for _ in range(k):
                    j = mill_state["i"]
                    if j >= len(mill_jobs):
                        return
                    mill_state["i"] = j + 1
                    srcb, dstb, s = mill_jobs[j]
                    sl = j % 2
                    ld = 0
                    lw = [(sT, mill_state["in_free"][sl])]
                    for c in range(4):
                        ld = dma_in(rMILL[sl],
                                    ml_in[sl][:, c * 512:(c + 1) * 512],
                                    srcb[s, c, :, :], lw)
                        lw = ()
                    tmm = 0
                    for b4 in range(4):
                        pb = b4 % 2
                        psb = mlb[pb][:, 0:512]
                        w0 = [(rMILL[sl], ld), (sS, sI),
                              (sV, mill_state["ps_free"][pb])]
                        for k4 in range(4):
                            hc = b4 * 4 + k4
                            tmm = onT(lambda e, hc=hc, k4=k4, psb=psb, sl=sl:
                                      e.transpose(
                                out=psb[:, k4 * 128:(k4 + 1) * 128],
                                in_=ml_in[sl][:, hc * 128:(hc + 1) * 128],
                                identity=ident[:]), w0)
                            w0 = ()
                        ev = onV(lambda e, b4=b4, psb=psb, sl=sl: e.tensor_copy(
                            out=ml_st[sl][:, b4 * 512:(b4 + 1) * 512],
                            in_=psb[:, 0:512]),
                            [(sT, tmm), (wMILL[sl], mill_state["st_free"][sl])])
                        mill_state["ps_free"][pb] = ev
                    mill_state["in_free"][sl] = tmm
                    wv = dma_wr(wMILL[sl], dstb[s, :, :], ml_st[sl][:],
                                [(sV, mill_state["ps_free"][0]),
                                 (sV, mill_state["ps_free"][1])])
                    mill_state["st_free"][sl] = wv

            wo_free = 0
            po_free = [0, 0]
            wr_attn = 0
            mm_l = 0
            for hh in range(2):
                lw = [(sT, wo_free)]
                for fc in range(16):
                    dma_in(rWO, wo_hi[:, fc * 1024:fc * 1024 + 512],
                           d_swo_hi[fc, hh * 2, :, :], lw)
                    lw = ()
                    dma_in(rWO, wo_hi[:, fc * 1024 + 512:(fc + 1) * 1024],
                           d_swo_hi[fc, hh * 2 + 1, :, :])
                    dma_in(rWO, wo_lo[:, fc * 1024:fc * 1024 + 512],
                           d_swo_lo[fc, hh * 2, :, :])
                    dma_in(rWO, wo_lo[:, fc * 1024 + 512:(fc + 1) * 1024],
                           d_swo_lo[fc, hh * 2 + 1, :, :])
                wo_ready = rWO.n
                vt_free = [0, 0]
                for tt in range(NTT):
                    vb = tt % 2
                    lw = [(sT, vt_free[vb])]
                    for f in range(16):
                        dma_in(rVT[vb], vt_hi[vb][:, f * 128:(f + 1) * 128],
                               d_sval_hi[tt, f, :, :], lw)
                        lw = ()
                        dma_in(rVT[vb], vt_lo[vb][:, f * 128:(f + 1) * 128],
                               d_sval_lo[tt, f, :, :])
                    vready = rVT[vb].n
                    for hcB in range(2):
                        pb = hcB
                        wq = [(rVT[vb], vready), (rWO, wo_ready),
                              (sV, po_free[pb])]
                        for fc in range(16):
                            for pi, (vv_, ww_) in enumerate(
                                    ((vt_hi[vb], wo_hi), (vt_hi[vb], wo_lo),
                                     (vt_lo[vb], wo_hi))):
                                mm_l = onT(lambda e, vv_=vv_, ww_=ww_, fc=fc,
                                           hcB=hcB, pb=pb,
                                           st=(fc == 0 and pi == 0),
                                           sp=(fc == 15 and pi == 2): e.matmul(
                                    out=P_A[:, pb * 512:(pb + 1) * 512],
                                    lhsT=vv_[:, fc * 128:(fc + 1) * 128],
                                    rhs=ww_[:, fc * 1024 + hcB * 512:
                                            fc * 1024 + (hcB + 1) * 512],
                                    start=st, stop=sp), wq)
                                wq = ()
                        pv = onV(lambda e, hcB=hcB, pb=pb: e.tensor_copy(
                            out=ao[:, hcB * 512:(hcB + 1) * 512],
                            in_=P_A[:, pb * 512:(pb + 1) * 512]),
                            [(sT, mm_l), (wATTN, wr_attn)])
                        po_free[pb] = pv
                    vt_free[vb] = mm_l
                    dma_wr(wATTN, d_sattnB[tt, hh * 2, :, :], ao[:, 0:512],
                           [(sV, po_free[0]), (sV, po_free[1])])
                    wr_attn = dma_wr(wATTN, d_sattnB[tt, hh * 2 + 1, :, :],
                                     ao[:, 512:1024], ())
                    emit_mill_steps(8)
                wo_free = mm_l
            emit_mill_steps(len(mill_jobs))

        barrier()

        with contextlib.ExitStack() as ph_c:
            xqg_hi = sb("xqg_hi", [128, 8192], BF16, ph_c)
            xqg_lo = sb("xqg_lo", [128, 8192], BF16, ph_c)
            ws_w = [[sb(f"wsC{i}_{j}", [128, 2048], BF16, ph_c)
                     for j in range(4)] for i in range(2)]
            hT = [sb(f"hT{i}", [128, 512], F32, ph_c) for i in range(2)]
            hstage = [[sb(f"hstg{g}_{i}", [128, 512], F32, ph_c)
                       for i in range(4)] for g in range(2)]
            hqs_hi = sb("hqs_hi", [128, 512], BF16, ph_c)
            hqs_lo = sb("hqs_lo", [128, 512], BF16, ph_c)
            gtmp = sb("gtmp", [128, 512], F32, ph_c)
            dml_in = [sb(f"dml_in{i}", [128, 2048], BF16, ph_c) for i in range(2)]
            dml_st = [sb(f"dml_st{i}", [128, 2048], BF16, ph_c) for i in range(2)]
            c_t = sb("c_t", [128, QL], F32, ph_c)
            c_u = sb("c_u", [128, QL], F32, ph_c)
            c_f = sb("c_f", [128, QL], F32, ph_c)
            c_strip = sb("c_strip", [128, 2048], F32, ph_c)
            tbf = [ph_c.enter_context(
                nc.psum_tensor(f"TBF{i}", [128, 1024], BF16)) for i in range(2)]
            htf = [ph_c.enter_context(
                nc.psum_tensor(f"HTF{i}", [128, 512], F32)) for i in range(2)]
            bf_ctr = [0]
            bf_free = [0, 0]

            prep_env["strip"] = c_strip
            prep_env["ws"] = {"t": c_t[:], "u": c_u[:], "f": c_f[:]}
            prep_env["ld_extra"] = lambda: []

            rXQG = stream("rXQG")
            rGWS = [stream("rGWS0"), stream("rGWS1")]
            rDML = [stream("rDML0"), stream("rDML1")]
            wDML = [stream("wDML0"), stream("wDML1")]
            wHQT = stream("wHQT")

            xqg_w_hi = xqg_hi[:].rearrange("p (a c4 q) -> p a c4 q", a=4, c4=4)
            xqg_w_lo = xqg_lo[:].rearrange("p (a c4 q) -> p a c4 q", a=4, c4=4)

            dml_jobs = [(srcb, dstb, idx)
                        for (srcb, dstb) in ((d_sdw_hi, d_dwT_hi),
                                             (d_sdw_lo, d_dwT_lo))
                        for idx in range(64)]
            dml_state = {"i": 0, "in_free": [0, 0], "st_free": [0, 0],
                         "ps_free": [0, 0]}

            def emit_dml_steps(k):
                for _ in range(k):
                    j = dml_state["i"]
                    if j >= len(dml_jobs):
                        return
                    dml_state["i"] = j + 1
                    srcb, dstb, idx = dml_jobs[j]
                    s_i, c4 = idx // 4, idx % 4
                    sl = j % 2
                    ld = 0
                    lw = [(sT, dml_state["in_free"][sl]),
                          (wQ, prep_state["wr"][("dw", idx)])]
                    for c in range(4):
                        ld = dma_in(rDML[sl],
                                    dml_in[sl][:, c * 512:(c + 1) * 512],
                                    srcb[s_i, c4 * 4 + c, :, :], lw)
                        lw = ()
                    tmm = 0
                    ev = 0
                    for b4 in range(4):
                        r = bf_ctr[0] % 2
                        bf_ctr[0] += 1
                        psb = tbf[r][:, 0:512]
                        w0 = [(rDML[sl], ld), (sS, sI), (sV, bf_free[r])]
                        for k4 in range(4):
                            icc = b4 * 4 + k4
                            tmm = onT(lambda e, icc=icc, k4=k4, psb=psb, sl=sl:
                                      e.transpose(
                                out=psb[:, k4 * 128:(k4 + 1) * 128],
                                in_=dml_in[sl][:, icc * 128:(icc + 1) * 128],
                                identity=ident[:]), w0)
                            w0 = ()
                        ev = onV(lambda e, b4=b4, psb=psb, sl=sl: e.tensor_copy(
                            out=dml_st[sl][:, b4 * 512:(b4 + 1) * 512],
                            in_=psb[:, 0:512]),
                            [(sT, tmm), (wDML[sl], dml_state["st_free"][sl])])
                        bf_free[r] = ev
                    dml_state["in_free"][sl] = tmm
                    wv = 0
                    evw = [(sV, ev)]
                    for icc in range(16):
                        wv = dma_wr(wDML[sl], dstb[c4 * 16 + icc, s_i, :, :],
                                    dml_st[sl][:, icc * 128:(icc + 1) * 128],
                                    evw)
                        evw = ()
                    dml_state["st_free"][sl] = wv

            wch_free = [0, 0]
            gate_free = [0, 0]
            up_free = [0, 0]
            vh_last = 0
            hq_ps_free = [0, 0]
            hq_ps_ctr = [0]
            ht_ps_free = [0, 0]
            ht_ctr = [0]
            hT_free = [0, 0]
            hst_gate = [[0] * 4 for _ in range(2)]
            hst_ev = [[0] * 4 for _ in range(2)]
            hqs_wr = [0]
            grp_free = 0
            iter_ctr = 0
            strip_seq = [0]
            ready = []
            pend = []

            def hq_tp4(tiles, idap, gate, bf=True):
                if bf:
                    r = bf_ctr[0] % 2
                    bf_ctr[0] += 1
                    psv = tbf[r][:, 0:512]
                    free = bf_free
                else:
                    r = ht_ctr[0] % 2
                    ht_ctr[0] += 1
                    psv = htf[r][:, 0:256]
                    free = ht_ps_free
                w0 = list(gate) + [(sS, sI), (sV, free[r])]
                tmm = 0
                for k4, tile in enumerate(tiles):
                    tmm = onT(lambda e, t_=tile, k4=k4, psv=psv: e.transpose(
                        out=psv[:, k4 * 128:(k4 + 1) * 128],
                        in_=t_, identity=idap), w0)
                    w0 = ()
                return tmm, psv, r

            def flush_tp(ent):
                g4d, icd, tt4d, shi, vlo, _ = ent
                qtm, psv, r = hq_tp4(
                    [q_hi[:, k * 128:(k + 1) * 128] for k in range(4)],
                    ident[:], [(sS, shi), (wHQT, hqs_wr[0])])
                evh = onV(lambda e, psv=psv: e.tensor_copy(
                    out=hqs_hi[:], in_=psv[:, 0:512]), [(sT, qtm)])
                bf_free[r] = evh
                qtm, psv, r = hq_tp4(
                    [q_lo[:, k * 128:(k + 1) * 128] for k in range(4)],
                    ident[:], [(sV, vlo), (wHQT, hqs_wr[0])])
                evl = onV(lambda e, psv=psv: e.tensor_copy(
                    out=hqs_lo[:], in_=psv[:, 0:512]), [(sT, qtm)])
                bf_free[r] = evl
                qws["w"] = [(sT, qtm)]
                dma_wr(wHQT, d_hqT_hi[g4d * 4 + tt4d, icd, :, :],
                       hqs_hi[:], [(sV, evh)])
                hqs_wr[0] = dma_wr(wHQT, d_hqT_lo[g4d * 4 + tt4d, icd, :, :],
                                   hqs_lo[:], [(sV, evl)])

            def flush_pend():
                while pend and pend[0][5] < strip_seq[0]:
                    flush_tp(pend.pop(0))

            def emit_one_chain():
                if ready:
                    g4d, icd, tt4d, buf = ready.pop(0)
                    shi, vlo = emit_quant(hstage[buf][tt4d][:],
                                          prep_env["ws"],
                                          [(sV, hst_ev[buf][tt4d])])
                    hst_gate[buf][tt4d] = shi
                    pend.append((g4d, icd, tt4d, shi, vlo, strip_seq[0]))
                strip_seq[0] += 1

            for g4 in range(4):
                lw = [(sT, grp_free)]
                for tt4 in range(4):
                    for hf in range(4):
                        dma_in(rXQG,
                               xqg_w_hi[:, hf, :, tt4 * 128:(tt4 + 1) * 128],
                               d_xqT_hi[g4 * 4 + tt4, hf, :, :].rearrange(
                                   "p (c t) -> p c t", t=128), lw)
                        lw = ()
                        dma_in(rXQG,
                               xqg_w_lo[:, hf, :, tt4 * 128:(tt4 + 1) * 128],
                               d_xqT_lo[g4 * 4 + tt4, hf, :, :].rearrange(
                                   "p (c t) -> p c t", t=128))
                xq_ready = rXQG.n
                mm_l = 0
                for istrip in range(64):
                    pb = istrip % 2
                    lw = [(sT, wch_free[pb])]
                    for j, dsp in enumerate((d_gwT_hi[istrip], d_gwT_lo[istrip],
                                             d_uwT_hi[istrip],
                                             d_uwT_lo[istrip])):
                        dma_in(rGWS[pb], ws_w[pb][j][:], dsp, lw)
                        lw = ()
                    wready = rGWS[pb].n
                    for half, (w_h, w_l) in enumerate(
                            ((ws_w[pb][0], ws_w[pb][1]),
                             (ws_w[pb][2], ws_w[pb][3]))):
                        bk = pb if half == 0 else 2 + pb
                        wq = [(rGWS[pb], wready), (rXQG, xq_ready)]
                        wq.append((sS, gate_free[pb]) if half == 0
                                  else (sV, up_free[pb]))
                        for hc in range(16):
                            for pi, (wt, xf) in enumerate(
                                    ((w_h, xqg_hi), (w_h, xqg_lo),
                                     (w_l, xqg_hi))):
                                mm_l = onT(lambda e, wt=wt, xf=xf, hc=hc,
                                           bk=bk, st=(hc == 0 and pi == 0),
                                           sp=(hc == 15 and pi == 2): e.matmul(
                                    out=P_A[:, bk * 512:(bk + 1) * 512],
                                    lhsT=wt[:, hc * 128:(hc + 1) * 128],
                                    rhs=xf[:, hc * 512:(hc + 1) * 512],
                                    start=st, stop=sp), wq)
                                wq = ()
                    wch_free[pb] = mm_l
                    sl = istrip % 2
                    sg = onS(lambda e, pb=pb: e.activation(
                        out=gtmp[:], in_=P_A[:, pb * 512:(pb + 1) * 512],
                        func=Act.Silu), [(sT, mm_l), (sV, vh_last)])
                    gate_free[pb] = sg
                    vh = onV(lambda e, pb=pb, sl=sl: e.tensor_tensor(
                        out=hT[sl][:], in0=gtmp[:],
                        in1=P_A[:, (2 + pb) * 512:(3 + pb) * 512],
                        op=Alu.mult), [(sS, sg), (sT, hT_free[sl])])
                    up_free[pb] = vh
                    vh_last = vh
                    i4 = istrip % 4
                    ic = istrip // 4
                    buf = ic % 2
                    tmm = 0
                    ev = 0
                    for tt4 in range(4):
                        tmm, psv, r = hq_tp4(
                            [hT[sl][:, tt4 * 128:(tt4 + 1) * 128]],
                            identF[:], [(sV, vh)], bf=False)
                        ev = onV(lambda e, tt4=tt4, i4=i4, buf=buf, psv=psv:
                                 e.tensor_copy(
                            out=hstage[buf][tt4][:, i4 * 128:(i4 + 1) * 128],
                            in_=psv[:, 0:128]),
                            [(sT, tmm)]
                            + ([(sS, hst_gate[buf][tt4])] if i4 == 0 else []))
                        ht_ps_free[r] = ev
                        hst_ev[buf][tt4] = ev
                    hT_free[sl] = tmm
                    if i4 == 3:
                        for tt4 in range(4):
                            ready.append((g4, ic, tt4, buf))
                    flush_pend()
                    if iter_ctr < 64:
                        emit_prep_steps(1)
                    elif iter_ctr < 192:
                        emit_dml_steps(1)
                    iter_ctr += 1
                    emit_one_chain()
                grp_free = mm_l
            while ready or pend:
                flush_pend()
                emit_one_chain()
            emit_prep_steps(len(prep_jobs))
            emit_dml_steps(len(dml_jobs))

        barrier()

        with contextlib.ExitStack() as ph_d:
            dwc_hi = [sb(f"dwch{i}", [128, 2048], BF16, ph_d) for i in range(2)]
            dwc_lo = [sb(f"dwcl{i}", [128, 2048], BF16, ph_d) for i in range(2)]
            hq_h = [sb(f"hqh{i}", [128, 4096], BF16, ph_d) for i in range(2)]
            hq_l = [sb(f"hql{i}", [128, 4096], BF16, ph_d) for i in range(2)]
            gsl = sb("gsl", [128, 512], F32, ph_d)
            osl = sb("osl", [128, 512], F32, ph_d)
            ssl = [sb(f"ssl{i}", [128, 512], F32, ph_d) for i in range(2)]
            xsl = [sb(f"xsl{i}", [128, 512], F32, ph_d) for i in range(2)]

            pd2 = ph_d.enter_context(
                nc.psum_tensor("PD2", [128, 2048], F32))

            rDW = [stream("rDW0"), stream("rDW1")]
            rXD = [stream("rXD0"), stream("rXD1")]
            wOUT = stream("wOUT")

            def PPbank(t8):
                PP = P_A if t8 < 4 else pd2
                return PP[:, (t8 % 4) * 512:((t8 % 4) + 1) * 512]

            dw_free = [0, 0]
            bank_sfree = [0] * 8
            out_wr = 0
            vo1_hist = [0, 0]
            vo2_hist = [0, 0]

            def emit_xs_loads(tkg, hg, t8):
                tt = tkg * 8 + t8
                sl = t8 % 2
                lx = dma_in(rXD[sl], xsl[sl][:],
                            d_x[tt * 128:(tt + 1) * 128,
                                hg * 512:(hg + 1) * 512], [(sV, vo2_hist[sl])])
                ls = dma_in(rXD[sl], ssl[sl][:], d_sattnB[tt, hg, :, :],
                            [(sV, vo1_hist[sl])])
                return lx, ls, sl

            for tkg in range(2):
                for hg in range(4):
                    mm_l = 0
                    for ic in range(16):
                        cb = ic % 2
                        lw = [(sT, dw_free[cb])]
                        for isub in range(4):
                            istrip = ic * 4 + isub
                            for ht in range(4):
                                dma_in(rDW[cb],
                                       dwc_hi[cb][:, isub * 512 + ht * 128:
                                                  isub * 512 + (ht + 1) * 128],
                                       d_dwT_hi[istrip, hg * 4 + ht, :, :], lw)
                                lw = ()
                                dma_in(rDW[cb],
                                       dwc_lo[cb][:, isub * 512 + ht * 128:
                                                  isub * 512 + (ht + 1) * 128],
                                       d_dwT_lo[istrip, hg * 4 + ht, :, :])
                        for t8 in range(8):
                            dma_in(rDW[cb],
                                   hq_h[cb][:, t8 * 512:(t8 + 1) * 512],
                                   d_hqT_hi[tkg * 8 + t8, ic, :, :])
                            dma_in(rDW[cb],
                                   hq_l[cb][:, t8 * 512:(t8 + 1) * 512],
                                   d_hqT_lo[tkg * 8 + t8, ic, :, :])
                        dready = rDW[cb].n
                        first = True
                        for t8 in range(8):
                            for isub in range(4):
                                for pi, (hh_, dd_) in enumerate(
                                        ((hq_h[cb], dwc_hi[cb]),
                                         (hq_h[cb], dwc_lo[cb]),
                                         (hq_l[cb], dwc_hi[cb]))):
                                    st_f = (ic == 0 and isub == 0 and pi == 0)
                                    wq = []
                                    if first:
                                        wq.append((rDW[cb], dready))
                                        first = False
                                    if st_f:
                                        wq.append((sS, bank_sfree[t8]))
                                    mm_l = onT(lambda e, hh_=hh_, dd_=dd_,
                                               isub=isub, t8=t8, st=st_f,
                                               sp=(ic == 15 and isub == 3
                                                   and pi == 2): e.matmul(
                                        out=PPbank(t8),
                                        lhsT=hh_[:, t8 * 512 + isub * 128:
                                                 t8 * 512 + (isub + 1) * 128],
                                        rhs=dd_[:, isub * 512:
                                                (isub + 1) * 512],
                                        start=st, stop=sp), wq)
                        dw_free[cb] = mm_l
                    lx, ls, lsl = emit_xs_loads(tkg, hg, 0)
                    for t8 in range(8):
                        tt = tkg * 8 + t8
                        sl = t8 % 2
                        nlx = nls = 0
                        if t8 < 7:
                            nlx, nls, _ = emit_xs_loads(tkg, hg, t8 + 1)
                        sg_d = onS(lambda e, t8=t8: e.activation(
                            out=gsl[:], in_=PPbank(t8), func=Act.Silu),
                            [(sT, mm_l), (sV, vo1_hist[0]),
                             (sV, vo1_hist[1])])
                        bank_sfree[t8] = sg_d
                        vo1 = onV(lambda e, sl=sl: e.tensor_tensor(
                            out=osl[:], in0=gsl[:], in1=ssl[sl][:],
                            op=Alu.mult),
                            [(sS, sg_d), (rXD[sl], ls), (wOUT, out_wr)])
                        vo2 = onV(lambda e, sl=sl: e.tensor_tensor(
                            out=osl[:], in0=osl[:], in1=xsl[sl][:],
                            op=Alu.add), [(rXD[sl], lx)])
                        out_wr = dma_wr(wOUT,
                                        d_out[tt * 128:(tt + 1) * 128,
                                              hg * 512:(hg + 1) * 512],
                                        osl[:], [(sV, vo2)])
                        vo1_hist[sl] = vo1
                        vo2_hist[sl] = vo2
                        if t8 < 7:
                            lx, ls = nlx, nls

        P.on("sync", None, [(s_, s_.n) for s_ in _streams.values()])

        with contextlib.ExitStack() as sem_stack:
            for s_ in all_sems:
                s_.h = sem_stack.enter_context(nc.semaphore(s_.name))
            block = sem_stack.enter_context(nc.Block())

            def run_queue(eng_name):
                def body(eng):
                    for (fn, waits, inc, inc_n) in P.q[eng_name]:
                        for (s_, v) in waits:
                            eng.wait_ge(s_.h, v)
                        if fn is not None:
                            ins = fn(eng)
                            if inc is not None:
                                ins.then_inc(inc.h, inc_n)
                        elif inc is not None:
                            eng.nop().then_inc(inc.h, inc_n)
                return body

            block.sync(run_queue("sync"))
            block.vector(run_queue("vector"))
            block.scalar(run_queue("scalar"))
            block.tensor(run_queue("tensor"))
            block.gpsimd(run_queue("gpsimd"))

    return nc


_NC_CACHE = {}


def kernel(L0_attn_x, L0_attn_w_qkv, L0_attn_w_o, L0_ffn_gate_w, L0_ffn_up_w,
           L0_ffn_down_w):
    x = np.ascontiguousarray(np.asarray(L0_attn_x, dtype=np.float32))
    wqkv = np.ascontiguousarray(np.asarray(L0_attn_w_qkv, dtype=np.float32))
    wo = np.ascontiguousarray(np.asarray(L0_attn_w_o, dtype=np.float32))
    gw = np.ascontiguousarray(np.asarray(L0_ffn_gate_w, dtype=np.float32))
    uw = np.ascontiguousarray(np.asarray(L0_ffn_up_w, dtype=np.float32))
    dw = np.ascontiguousarray(np.asarray(L0_ffn_down_w, dtype=np.float32))
    cst = np.tile(np.array([CST_VALS], np.float32), (128, 1))
    ones1 = np.ones((1, 128), np.float32)
    idf = np.eye(128, dtype=np.float32)

    if "nc" not in _NC_CACHE:
        _NC_CACHE["nc"] = build_program()
    nc = _NC_CACHE["nc"]
    in_maps = []
    for c in range(8):
        in_maps.append({"x": x[c], "wqkv": wqkv, "wo": wo, "gw": gw, "uw": uw,
                        "dw": dw, "cst": cst, "ones1": ones1, "identf": idf})
    res = run_bass_kernel_spmd(nc, in_maps, core_ids=list(range(8)))
    return np.stack([res.results[c]["out"] for c in range(8)], axis=0)
